# version 14
# baseline (speedup 1.0000x reference)
import numpy as np
import ml_dtypes
import concourse.bass as bass
import concourse.mybir as mybir
from concourse.bass_utils import run_bass_kernel_spmd
from contextlib import ExitStack

F32 = mybir.dt.float32
BF16 = mybir.dt.bfloat16
AF = mybir.ActivationFunctionType
ALU = mybir.AluOpType
BF = ml_dtypes.bfloat16

D, H, T, TT = 1024, 4096, 2048, 512
NT, NDS, NHS = T // TT, D // 128, H // 128
EPS = 1e-5
ENGS = ("sync", "gpsimd", "tensor", "scalar", "vector")
DBG = False


class Sem:
    def __init__(self, nc, name):
        self.h = nc.alloc_semaphore(name=name)
        self.v = 0


class Em:
    def __init__(self):
        self.streams = {e: [] for e in ENGS}

    def on(self, eng, fn):
        self.streams[eng].append(fn)

    def wait(self, eng, sem, val):
        if isinstance(val, tuple):
            tag, v = val
            sem2 = {"S": self.sS, "B": self.sB, "V": self.sV}[tag]
            if v > 0:
                self.on(eng, lambda e, s=sem2.h, vv=v: e.wait_ge(s, vv))
            return
        if val > 0:
            self.on(eng, lambda e, s=sem.h, v=val: e.wait_ge(s, v))

    def emit(self, eng, sem, n, fn):
        if sem is None:
            self.on(eng, lambda e, f=fn: f(e))
            return None
        sem.v += n
        self.on(eng, lambda e, f=fn, s=sem.h, m=n: f(e).then_inc(s, m))
        return sem.v

    def run(self, block):
        for eng in ENGS:
            fns = self.streams[eng]
            getattr(block, eng)(lambda e, fns=fns: [f(e) for f in fns])


def build(dbg=False):
    nc = bass.Bass("TRN2", target_bir_lowering=False)

    xT = nc.dram_tensor("xT", [D, T], F32, kind="ExternalInput")
    WQ = nc.dram_tensor("wq", [D, D], F32, kind="ExternalInput")
    WK = nc.dram_tensor("wk", [D, D], F32, kind="ExternalInput")
    WV = nc.dram_tensor("wv", [D, D], F32, kind="ExternalInput")
    WO = nc.dram_tensor("wo", [D, D], F32, kind="ExternalInput")
    BQ = nc.dram_tensor("bq", [D], F32, kind="ExternalInput")
    BK = nc.dram_tensor("bk", [D], F32, kind="ExternalInput")
    BV = nc.dram_tensor("bv", [D], F32, kind="ExternalInput")
    BO = nc.dram_tensor("bo", [D], F32, kind="ExternalInput")
    W1T = nc.dram_tensor("w1t", [D, H], F32, kind="ExternalInput")
    W2T = nc.dram_tensor("w2t", [H, D], F32, kind="ExternalInput")
    B1 = nc.dram_tensor("b1", [H], F32, kind="ExternalInput")
    B2 = nc.dram_tensor("b2", [D], F32, kind="ExternalInput")
    DPAR = nc.dram_tensor("dpar", [2], F32, kind="ExternalInput")
    EXPJ = nc.dram_tensor("expj", [TT], F32, kind="ExternalInput")
    EKD = nc.dram_tensor("ekd", [TT], F32, kind="ExternalInput")
    EXPT = nc.dram_tensor("expt", [TT, TT], F32, kind="ExternalInput")
    LT01 = nc.dram_tensor("lt01", [TT, TT], F32, kind="ExternalInput")
    EYE = nc.dram_tensor("eye", [128, 128], BF16, kind="ExternalInput")

    outT = nc.dram_tensor("outT", [D, T], F32, kind="ExternalOutput")

    sk = "ExternalOutput" if dbg else "Internal"
    d_xmid = nc.dram_tensor("d_xmid", [D, T], F32, kind=sk)
    d_xpost = nc.dram_tensor("d_xpost", [D, T], F32, kind=sk)
    d_read = nc.dram_tensor("d_read", [D, T], F32, kind=sk)
    d_qdt = nc.dram_tensor("d_qdt", [D, T], BF16, kind=sk)
    d_s1t = nc.dram_tensor("d_s1t", [D, H], BF16)
    d_s2t = nc.dram_tensor("d_s2t", [H, D], BF16)
    d_wb = nc.dram_tensor("d_wb", [4 * D, D], BF16)
    d_rows = nc.dram_tensor("d_rows", [4, TT], F32)
    d_small = nc.dram_tensor("d_small", [16], F32)
    cc_in = nc.dram_tensor("cc_in", [D, D], BF16)
    cc_out = nc.dram_tensor("cc_out", [2 * D, D], BF16)
    POFF = nc.dram_tensor("poff", [1], mybir.dt.int32, kind="ExternalInput")

    with ExitStack() as ctx:
        E = ctx.enter_context
        AW = 52750
        arena = E(nc.sbuf_tensor("arena", [128, AW], F32))
        poff_t = E(nc.sbuf_tensor("poff_t", [1, 1], mybir.dt.int32))
        ar = arena.ap()

        def reg(off, words, dtype=F32):
            assert off + words <= AW, (off, words)
            a = ar[:, off:off + words]
            return a.bitcast(BF16) if dtype == BF16 else a

        ones_bf = reg(0, 64, BF16)
        alpha1g = reg(64, 32)
        b1t = reg(96, 32)
        alpha2h = reg(128, 8)
        b2h = reg(136, 8)
        small = reg(144, 16)
        rows_r = reg(160, 1536)
        P_bq = reg(1696, 8)
        P_bk = reg(1704, 8)
        P_bo = reg(1712, 8)
        PB = 1720
        C_LOGD, C_FLAG, C_DECC, C_SCV = 0, 1, 2, 4
        sl0 = rows_r[0:1, 0:TT]
        bchl = rows_r[0:1, 0:512].bitcast(BF16)
        sl1 = rows_r[0:1, TT:2 * TT]
        sl2 = rows_r[0:1, 2 * TT:3 * TT]
        r_dec, r_ekd = sl0, sl1
        r_dp = rows_r[0:1, 2 * TT:2 * TT + 8]
        r_bar = rows_r[0:1, 3 * TT - 1:3 * TT]
        poff_sb = poff_t[0:1, 0:1]

        F_s1T = reg(PB + 0, 16384, BF16)
        F_s2T = reg(PB + 16384, 16384, BF16)
        F_xcb = reg(PB + 32768, 2048, BF16)
        F_xsq = reg(PB + 34816, 2048, BF16)
        F_xhat = reg(PB + 36864, 2048, BF16)
        F_h = reg(PB + 38912, 8192, BF16)
        F_ring = reg(PB + 47104, 1024)
        F_abc = reg(PB + 48128, 512)
        F_bbc = reg(PB + 48640, 512)
        F_tmp = reg(PB + 49152, 512)
        wstgP = [reg(PB + 38912, 2048, BF16), reg(PB + 40960, 2048, BF16)]
        awP = [reg(PB + 43008, 2048, BF16), reg(PB + 45056, 2048, BF16)]

        A_wq = reg(PB + 0, 4096, BF16)
        A_wk = reg(PB + 4096, 4096, BF16)
        A_wv = reg(PB + 8192, 4096, BF16)
        A_qT = reg(PB + 12288, 2048, BF16)
        A_kT = reg(PB + 14336, 2048, BF16)
        A_wo = reg(PB + 16384, 4096, BF16)
        A_M = reg(PB + 20480, 8192)
        A_Mb = reg(PB + 28672, 4096, BF16)
        A_xmcb = reg(PB + 32768, 2048, BF16)
        A_xsq = reg(PB + 34816, 2048, BF16)
        A_xhat = reg(PB + 36864, 2048, BF16)
        A_kd = reg(PB + 38912, 2048, BF16)
        A_qdT = reg(PB + 40960, 2048, BF16)
        A_at = reg(PB + 43008, 640, BF16)
        A_rstg = reg(PB + 43648, 1024)
        A_v = reg(PB + 44672, 2048, BF16)
        A_tmp = reg(PB + 46720, 512)
        A_tmp2 = reg(PB + 47232, 512)
        A_deckd = reg(PB + 47744, 4)
        A_eye = reg(PB + 47748, 64, BF16)
        A_abc = F_abc
        A_bbc = F_bbc
        A_maskT = reg(PB + 49152, 1024, BF16)
        A_decbc = reg(PB + 50176, 256, BF16)
        A_bvb = reg(PB + 50432, 512, BF16)
        A_bq, A_bk, A_bo = P_bq, P_bk, P_bo
        EXPT_sb = reg(PB + 20480, 2048)
        LT_sb = reg(PB + 22528, 2048)
        bvb_row = A_bvb[0:1, 0:1024]

        K_wo = A_wo
        K_m0 = reg(PB + 20480, 4096, BF16)
        K_rtg = [reg(PB + 24576, 4096), reg(PB + 28672, 4096)]
        K_xm = [reg(PB + 38912, 4096), reg(PB + 38912, 4096)]
        K_qdt = [reg(PB + 43008, 2048, BF16), reg(PB + 45056, 2048, BF16)]
        K_rfull = reg(PB + 48128, 2048, BF16)
        K_tmp = reg(PB + 50176, 512)

        pbk = [E(nc.psum_tensor(f"pb{i}", [128, 512], F32)) for i in range(4)]
        pTr = E(nc.psum_tensor("pTr", [128, 1024], BF16))
        pS1 = E(nc.psum_tensor("pS1", [1, 512], F32))
        pS2 = E(nc.psum_tensor("pS2", [1, 512], F32))
        pBC = E(nc.psum_tensor("pBC", [128, 512], F32))
        pbk.append(pBC)

        sT, sV, sS = Sem(nc, "sT"), Sem(nc, "sV"), Sem(nc, "sS")
        sX, sW, sB = Sem(nc, "sX"), Sem(nc, "sW"), Sem(nc, "sB")
        sG, sC = Sem(nc, "sG"), Sem(nc, "sC")
        sWq, sWk, sWv, sWo = (Sem(nc, n) for n in ("sWq", "sWk", "sWv", "sWo"))
        sR = [Sem(nc, "sR0"), Sem(nc, "sR1")]
        sRd = [Sem(nc, "sRd0"), Sem(nc, "sRd1"), Sem(nc, "sRd2"), Sem(nc, "sRd3")]
        sK = [Sem(nc, "sKa"), Sem(nc, "sKb")]
        sKx = Sem(nc, "sKx")

        em = Em()
        em.sB = sB
        em.sS = sS
        em.sV = sV

        def dma(eng, sem, out_ap, in_ap, slow=False):
            if slow:
                return em.emit(eng, sem, 16,
                               lambda e, ou=out_ap, ii=in_ap: e.dma_start(out=ou, in_=ii,
                                                                          allow_slow_non_contiguous=True))
            return em.emit(eng, sem, 16, lambda e, ou=out_ap, ii=in_ap: e.dma_start(out=ou, in_=ii))

        def mm(out_ap, lhsT, rhs, start, stop, inc=False):
            return em.emit("tensor", sT if inc else None, 1,
                           lambda e, ou=out_ap, l=lhsT, r=rhs, a=start, b=stop: e.matmul(ou, l, r, start=a, stop=b))

        def tr(out_ap, in_ap, inc=False):
            return em.emit("tensor", sT if inc else None, 1,
                           lambda e, ou=out_ap, ii=in_ap: e.transpose(ou, ii, A_eye))

        def vop(fn, inc=False):
            return em.emit("vector", sV if inc else None, 1, fn)

        def sop(fn, inc=False):
            return em.emit("scalar", sS if inc else None, 1, fn)

        ALL_DMA = (sX, sW, sB, sG, sWq, sWk, sWv, sWo, sR[0], sR[1], sRd[0], sRd[1], sRd[2], sRd[3], sK[0], sK[1], sKx)

        def barrier():
            sT.v += 1
            em.on("tensor", lambda e, s=sT.h: e.matmul(pS1[0:1, 0:1], ones_bf[:, 0:1], ones_bf[:, 0:1],
                                                       start=True, stop=True).then_inc(s, 1))
            vt = sT.v
            sS.v += 1
            em.on("scalar", lambda e, s=sS.h: e.copy(r_bar, r_bar).then_inc(s, 1))
            vs = sS.v
            sV.v += 1
            em.on("vector", lambda e, s=sV.h: e.engine_nop().then_inc(s, 1))
            vv = sV.v
            for eng in ENGS:
                if eng != "tensor":
                    em.wait(eng, sT, vt)
                if eng != "scalar":
                    em.wait(eng, sS, vs)
                if eng != "vector":
                    em.wait(eng, sV, vv)
                for dsem in ALL_DMA:
                    em.wait(eng, dsem, dsem.v)

        def dram_tile(t_dram, tok0, nds=NDS, ntok=TT, cols=T):
            return bass.AP(t_dram, tok0, [[cols, 128], [128 * cols, nds], [1, ntok]])

        def sb3(sb_ap, nds=NDS):
            return sb_ap.rearrange("p (a t) -> p a t", a=nds)

        def ln_rows(t_stats):
            em.wait("vector", sT, t_stats)
            vop(lambda e: e.tensor_copy(sl0, pS1[:, :]))
            vop(lambda e: e.tensor_copy(sl1, pS2[:, :]))
            vop(lambda e: e.scalar_tensor_tensor(sl2, sl0, -1.0 / D, sl0, ALU.mult, ALU.mult))
            vop(lambda e: e.tensor_tensor(sl2, sl1, sl2, ALU.add))
            vop(lambda e: e.tensor_scalar(sl2, sl2, 1.0 / D, EPS, ALU.mult, ALU.add))
            v2 = vop(lambda e: e.reciprocal(sl2, sl2), inc=True)
            em.wait("scalar", sV, v2)
            s1 = sop(lambda e: e.activation(sl2, sl2, AF.Sqrt), inc=True)
            em.wait("vector", sS, s1)
            return vop(lambda e: e.scalar_tensor_tensor(sl1, sl0, -1.0 / D, sl2, ALU.mult, ALU.mult), inc=True)

        def ln_bcast(v_rows, abc, bbc):
            em.wait("sync", sV, v_rows)
            dma("sync", sB, d_rows.ap()[0:1, :], sl2)
            dma("sync", sB, d_rows.ap()[1:2, :], sl1)
            em.wait("sync", sB, sB.v)
            dma("sync", sB, abc, d_rows.ap()[0:1, :].partition_broadcast(128))
            bcv = dma("sync", sB, bbc, d_rows.ap()[1:2, :].partition_broadcast(128))
            sV.v += 0
            return ("B", bcv)

        v_xcb0 = dma("gpsimd", sG, sb3(F_xcb), dram_tile(xT, 0))
        dma("sync", sX, r_dp[0:1, 0:2], DPAR.ap()[None, :])
        dma("sync", sX, poff_sb, POFF.ap()[None, :])
        dma("sync", sX, r_dec, EXPJ.ap()[None, :])
        dma("sync", sX, r_ekd, EKD.ap()[None, :])
        dma("sync", sX, b1t, bass.AP(B1, 0, [[1, 128], [128, 32]]), slow=True)
        dma("sync", sX, b2h, bass.AP(B2, 0, [[1, 128], [128, 8]]), slow=True)
        for t_d, dst in ((BQ, P_bq), (BK, P_bk), (BO, P_bo)):
            dma("sync", sX, dst, bass.AP(t_d, 0, [[1, 128], [128, 8]]), slow=True)
        v_ld0 = sX.v
        v_bvb0 = dma("gpsimd", sG, bvb_row, BV.ap()[None, :])

        em.wait("vector", sX, v_ld0)
        vop(lambda e: e.memset(ones_bf, 1.0))
        vop(lambda e: e.tensor_scalar(b2h, b2h, 0.5, None, ALU.mult), inc=True)

        em.wait("scalar", sX, v_ld0)
        sop(lambda e: e.activation(r_dp[0:1, 2:3], r_dp[0:1, 0:1], AF.Sigmoid))
        v_logd = sop(lambda e: e.activation(r_dp[0:1, 3:4], r_dp[0:1, 2:3], AF.Ln), inc=True)
        em.wait("sync", sS, v_logd)
        dma("sync", sB, d_small.ap()[None, 0:1], r_dp[0:1, 3:4])
        dma("sync", sB, d_small.ap()[None, 1:2], r_dp[0:1, 1:2])
        em.wait("sync", sB, sB.v)
        dma("sync", sB, small[:, C_LOGD:C_LOGD + 1], d_small.ap()[None, 0:1].partition_broadcast(128))
        v_bc0 = dma("sync", sB, small[:, C_FLAG:C_FLAG + 1], d_small.ap()[None, 1:2].partition_broadcast(128))

        em.wait("vector", sB, v_bc0)
        vop(lambda e: e.tensor_scalar(small[:, C_DECC:C_DECC + 1], small[:, C_LOGD:C_LOGD + 1], float(TT),
                                      None, ALU.mult))
        for i in range(4):
            vop(lambda e, i=i: e.tensor_scalar(small[:, C_SCV + i:C_SCV + i + 1], small[:, C_LOGD:C_LOGD + 1],
                                               float(TT * i), None, ALU.mult))
        vop(lambda e: e.tensor_scalar(r_dec, r_dec, small[0:1, C_LOGD:C_LOGD + 1], None, ALU.mult))
        v_pre = vop(lambda e: e.tensor_scalar(r_ekd, r_ekd, small[0:1, C_LOGD:C_LOGD + 1], None, ALU.mult),
                    inc=True)
        em.wait("scalar", sV, v_pre)
        sop(lambda e: e.activation(small[:, C_DECC:C_DECC + 1], small[:, C_DECC:C_DECC + 1], AF.Exp))
        for i in range(4):
            sop(lambda e, i=i: e.activation(small[:, C_SCV + i:C_SCV + i + 1], small[:, C_SCV + i:C_SCV + i + 1],
                                            AF.Exp))
        sop(lambda e: e.activation(r_dec, r_dec, AF.Exp))
        v_exp1 = sop(lambda e: e.activation(r_ekd, r_ekd, AF.Exp), inc=True)
        em.wait("vector", sS, v_exp1)
        for i in range(4):
            vop(lambda e, i=i: e.tensor_tensor(small[:, C_SCV + i:C_SCV + i + 1],
                                               small[:, C_SCV + i:C_SCV + i + 1],
                                               small[:, C_FLAG:C_FLAG + 1], ALU.mult), inc=(i == 3))
        em.wait("sync", sS, v_exp1)
        dma("sync", sB, d_rows.ap()[2:3, :], r_dec)
        v_decb = dma("sync", sB, d_rows.ap()[3:4, :], r_ekd)

        em.wait("scalar", sG, v_xcb0)
        s_sq0 = 0
        for q in range(4):
            s_sq0 = sop(lambda e, q=q: e.square(F_xsq[:, q * 1024:(q + 1) * 1024],
                                                F_xcb[:, q * 1024:(q + 1) * 1024]), inc=(q == 3))
        em.wait("tensor", sS, s_sq0)
        for ds in range(NDS):
            mm(pS1[:, :], ones_bf[:, 0:1], F_xcb[:, ds * TT:(ds + 1) * TT], start=(ds == 0),
               stop=(ds == NDS - 1))
        t_st0 = 0
        for ds in range(NDS):
            t_st0 = mm(pS2[:, :], ones_bf[:, 0:1], F_xsq[:, ds * TT:(ds + 1) * TT], start=(ds == 0),
                       stop=(ds == NDS - 1), inc=(ds == NDS - 1))
        em.wait("vector", sB, v_decb)
        v_r0 = ln_rows(t_st0)
        bc0 = ln_bcast(v_r0, F_abc, F_bbc)
        em.wait("vector", sV, bc0)
        v_ap0 = 0
        for ds in range(NDS):
            vop(lambda e, ds=ds: e.tensor_tensor(F_xhat[:, ds * TT:(ds + 1) * TT],
                                                 F_xcb[:, ds * TT:(ds + 1) * TT], F_abc, ALU.mult))
        for ds in range(NDS):
            v_ap0 = vop(lambda e, ds=ds: e.tensor_tensor(F_xhat[:, ds * TT:(ds + 1) * TT],
                                                         F_xhat[:, ds * TT:(ds + 1) * TT], F_bbc, ALU.add),
                        inc=(ds == NDS - 1))

        s_sign = {}
        v_aw = {}
        t_alv = {}
        first_al = [True, True]
        t_al1 = 0
        t_al2 = 0
        for ci in range(16):
            p = ci % 2
            w2g = ci >= 8
            g = ci - 8
            wst = wstgP[p]
            aw = awP[p]
            if ci >= 2:
                em.wait("gpsimd", sS, s_sign[ci - 2])
                em.wait("gpsimd", sV, v_aw[ci - 2])
            if not w2g:
                dma("gpsimd", sG, wst, bass.AP(W1T, ci * 128 * H, [[H, 128], [1, H]]))
            else:
                dma("gpsimd", sG, sb3(wst, 4), bass.AP(W2T, g * 512 * D, [[D, 128], [128 * D, 4], [1, D]]))
            v_w = sG.v
            em.wait("scalar", sG, v_w)
            if not w2g:
                s_sign[ci] = sop(lambda e, ci=ci, wst=wst: e.sign(F_s1T[:, ci * H:(ci + 1) * H], wst), inc=True)
            else:
                s_sign[ci] = sop(lambda e, g=g, wst=wst: e.sign(F_s2T[:, g * 4096:(g + 1) * 4096], wst),
                                 inc=True)
            em.wait("vector", sG, v_w)
            if w2g and g == 0:
                em.wait("vector", sT, t_al1)
                v_al1 = vop(lambda e: e.tensor_scalar(alpha1g, pbk[0][:, 0:32], 1.0 / D, None, ALU.mult), inc=True)
            v_aw[ci] = vop(lambda e, aw=aw, wst=wst: e.scalar_tensor_tensor(aw[:, 0:H], wst, -1.0, wst,
                                                                            ALU.mult, ALU.max), inc=True)
            em.wait("tensor", sV, v_aw[ci])
            if not w2g:
                for hs in range(NHS):
                    t_al1 = mm(pbk[0][:, hs:hs + 1], aw[:, hs * 128:(hs + 1) * 128], ones_bf[:, 0:1],
                               start=first_al[0], stop=(ci == 7 and hs == NHS - 1), inc=(hs == NHS - 1))
                    first_al[0] = False
            else:
                if g == 0:
                    em.wait("tensor", sV, v_al1)
                for k in range(4):
                    for ds in range(NDS):
                        t_al2 = mm(pbk[0][:, ds:ds + 1], aw[:, k * 1024 + ds * 128:k * 1024 + (ds + 1) * 128],
                                   ones_bf[:, 0:1], start=first_al[1],
                                   stop=(g == 7 and k == 3 and ds == NDS - 1), inc=(k == 3 and ds == NDS - 1))
                        first_al[1] = False
            em.wait("sync", sS, s_sign[ci])
            if not w2g:
                dma("sync", sW, bass.AP(d_s1t, ci * 128 * H, [[H, 128], [1, H]]), F_s1T[:, ci * H:(ci + 1) * H])
            else:
                dma("sync", sW, bass.AP(d_s2t, g * 512 * D, [[D, 128], [128 * D, 4], [1, D]]),
                    sb3(F_s2T[:, g * 4096:(g + 1) * 4096], 4))
        em.wait("vector", sT, t_al2)
        vop(lambda e: e.tensor_scalar(alpha2h, pbk[0][:, 0:8], 0.5 / H, None, ALU.mult), inc=True)

        def f_phase(xin, xout, reload_weights, bar=None, cast0_gate=None, pre0=None,
                    tail_hook_a=None, tail_hook_b=None, mid_hook=None):

            v_apply = {-1: 0}
            t_h_last = {-1: 0}
            v_xcb = {}
            b_bc = {}
            t_stats = {}
            bank_free_v = [0, 0, 0, 0, 0]
            bank_free_s = [0, 0, 0, 0, 0]
            slot_store = [sG.v, sG.v]

            def prep_cast(t):
                em.wait("gpsimd", sV, v_apply[t - 1])
                if t == 0 and cast0_gate is not None:
                    em.wait("gpsimd", cast0_gate[0], cast0_gate[1])
                v_xcb[t] = dma("gpsimd", sG, sb3(F_xcb), dram_tile(xin, t * TT))

            def prep_stats_v(t):
                em.wait("scalar", sG, v_xcb[t])
                s_sq = 0
                for q in range(4):
                    s_sq = sop(lambda e, q=q: e.square(F_xsq[:, q * 1024:(q + 1) * 1024],
                                                       F_xcb[:, q * 1024:(q + 1) * 1024]), inc=(q == 3))
                return s_sq

            def prep_stats_t(t, v_sq):
                em.wait("tensor", sS, v_sq)
                for ds in range(NDS):
                    mm(pS1[:, :], ones_bf[:, 0:1], F_xcb[:, ds * TT:(ds + 1) * TT], start=(ds == 0),
                       stop=(ds == NDS - 1))
                ts = 0
                for ds in range(NDS):
                    ts = mm(pS2[:, :], ones_bf[:, 0:1], F_xsq[:, ds * TT:(ds + 1) * TT], start=(ds == 0),
                            stop=(ds == NDS - 1), inc=(ds == NDS - 1))
                t_stats[t] = ts

            def prep_apply(t):
                em.wait("vector", sV, b_bc[t])
                em.wait("vector", sT, t_h_last[t - 1])
                va = 0
                for ds in range(NDS):
                    vop(lambda e, ds=ds: e.tensor_tensor(F_xhat[:, ds * TT:(ds + 1) * TT],
                                                         F_xcb[:, ds * TT:(ds + 1) * TT], F_abc, ALU.mult))
                for ds in range(NDS):
                    va = vop(lambda e, ds=ds: e.tensor_tensor(F_xhat[:, ds * TT:(ds + 1) * TT],
                                                              F_xhat[:, ds * TT:(ds + 1) * TT], F_bbc, ALU.add),
                             inc=(ds == NDS - 1))
                v_apply[t] = va

            if pre0 is None:
                prep_cast(0)
                v_sq0 = prep_stats_v(0)
                prep_stats_t(0, v_sq0)
                v_r0 = ln_rows(t_stats[0])
                b_bc[0] = ln_bcast(v_r0, F_abc, F_bbc)
                prep_apply(0)
            else:
                v_xcb[0] = pre0["xcb"]
                b_bc[0] = pre0["bc"]
                v_apply[0] = pre0["apply"]
            if bar is not None:
                bar()
            if reload_weights:
                for g in range(8):
                    dma("sync", sX, sb3(F_s2T[:, g * 4096:(g + 1) * 4096], 4),
                        bass.AP(d_s2t, g * 512 * D, [[D, 128], [128 * D, 4], [1, D]]))
            v_s2_ld = sX.v

            for t in range(NT):
                em.wait("tensor", sV, v_apply[t])
                for hs in range(NHS):
                    bi = hs % 5
                    em.wait("tensor", sS, bank_free_s[bi])
                    em.wait("tensor", sV, bank_free_v[bi])
                    t_h = 0
                    for ds in range(NDS):
                        t_h = mm(pbk[bi][:, :], F_s1T[:, ds * H + hs * 128:ds * H + (hs + 1) * 128],
                                 F_xhat[:, ds * TT:(ds + 1) * TT], start=(ds == 0), stop=(ds == NDS - 1),
                                 inc=(ds == NDS - 1))
                    em.wait("scalar", sT, t_h)
                    bank_free_s[bi] = sop(lambda e, hs=hs, bi=bi: e.activation(
                        F_h[:, hs * TT:(hs + 1) * TT], pbk[bi][:, :], AF.Gelu,
                        bias=b1t[:, hs:hs + 1], scale=alpha1g[:, hs:hs + 1]), inc=True)
                    if hs == 7:
                        if t + 1 < NT:
                            prep_cast(t + 1)
                            v_sq = prep_stats_v(t + 1)
                            prep_stats_t(t + 1, v_sq)
                            v_r = ln_rows(t_stats[t + 1])
                            b_bc[t + 1] = ln_bcast(v_r, F_abc, F_bbc)
                        elif tail_hook_a is not None:
                            tail_hook_a(v_apply[t])
                s_gelu_all = sS.v
                t_h_last[t] = sT.v

                for ds2 in range(2):
                    em.wait("sync", sG, slot_store[ds2 % 2])
                    dma("sync", sR[ds2 % 2], F_ring[:, (ds2 % 2) * 512:(ds2 % 2 + 1) * 512],
                        bass.AP(xin, ds2 * 128 * T + t * TT, [[T, 128], [1, TT]]))

                if reload_weights and t == 0:
                    em.wait("tensor", sX, v_s2_ld)
                em.wait("tensor", sS, s_gelu_all)
                for ds2 in range(NDS):
                    bi = ds2 % 5
                    em.wait("tensor", sS, bank_free_s[bi])
                    em.wait("tensor", sV, bank_free_v[bi])
                    t_o2 = 0
                    for hs in range(NHS):
                        t_o2 = mm(pbk[bi][:, :], F_s2T[:, hs * 1024 + ds2 * 128:hs * 1024 + (ds2 + 1) * 128],
                                  F_h[:, hs * TT:(hs + 1) * TT], start=(hs == 0), stop=(hs == NHS - 1),
                                  inc=(hs == NHS - 1))
                    sl = ds2 % 2
                    em.wait("vector", sT, t_o2)
                    em.wait("vector", sR[sl], sR[sl].v)
                    vop(lambda e, bi=bi, ds2=ds2: e.tensor_scalar(F_tmp, pbk[bi][:, :], alpha2h[:, ds2:ds2 + 1],
                                                                  b2h[:, ds2:ds2 + 1], ALU.mult, ALU.add))
                    v_ev = vop(lambda e, sl=sl: e.tensor_tensor(F_ring[:, sl * 512:(sl + 1) * 512],
                                                                F_ring[:, sl * 512:(sl + 1) * 512], F_tmp,
                                                                ALU.add), inc=True)
                    bank_free_v[bi] = v_ev
                    em.wait("gpsimd", sV, v_ev)
                    em.wait("gpsimd", sG, sG.v)
                    slot_store[sl] = dma("gpsimd", sG, bass.AP(xout, ds2 * 128 * T + t * TT, [[T, 128], [1, TT]]),
                                         F_ring[:, sl * 512:(sl + 1) * 512])
                    if ds2 + 2 < NDS:
                        em.wait("sync", sG, slot_store[sl])
                        dma("sync", sR[sl], F_ring[:, sl * 512:(sl + 1) * 512],
                            bass.AP(xin, (ds2 + 2) * 128 * T + t * TT, [[T, 128], [1, TT]]))
                    if ds2 == 1:
                        if t + 1 < NT:
                            prep_apply(t + 1)
                        elif tail_hook_b is not None:
                            tail_hook_b(t_h_last[t])
                    if ds2 == NDS - 1 and mid_hook is not None:
                        mid_hook(t)

        a_seed = {}
        WLIST = ((WQ, A_wq, sWq), (WK, A_wk, sWk), (WV, A_wv, sWv), (WO, A_wo, sWo))

        def w_d2d(i):
            a_seed[("d2d", i)] = dma("gpsimd", sG, bass.AP(d_wb, i * D * D, [[D, 128], [128 * D, NDS], [1, D]]),
                                     bass.AP(WLIST[i][0], 0, [[D, 128], [128 * D, NDS], [1, D]]))

        def a_mid_hook(t):
            if t < 3:
                w_d2d(t)

        def a_tail_hook_a(v_apply_last):
            w_d2d(3)
            em.wait("gpsimd", sV, v_apply_last)
            a_seed["xmcb"] = dma("gpsimd", sG, sb3(A_xmcb), dram_tile(d_xmid, 0))
            em.wait("vector", sG, a_seed["xmcb"])
            v_sq = 0
            for q in range(4):
                v_sq = vop(lambda e, q=q: e.tensor_tensor(A_xsq[:, q * 1024:(q + 1) * 1024],
                                                          A_xmcb[:, q * 1024:(q + 1) * 1024],
                                                          A_xmcb[:, q * 1024:(q + 1) * 1024], ALU.mult),
                           inc=(q == 3))
            em.wait("tensor", sV, v_sq)
            for ds in range(NDS):
                mm(pS1[:, :], ones_bf[:, 0:1], A_xmcb[:, ds * TT:(ds + 1) * TT], start=(ds == 0),
                   stop=(ds == NDS - 1))
            ts = 0
            for ds in range(NDS):
                ts = mm(pS2[:, :], ones_bf[:, 0:1], A_xsq[:, ds * TT:(ds + 1) * TT], start=(ds == 0),
                        stop=(ds == NDS - 1), inc=(ds == NDS - 1))
            v_r = ln_rows(ts)
            a_seed["bc"] = ln_bcast(v_r, A_abc, A_bbc)

        def a_tail_hook_b(t_h_val):
            em.wait("sync", sT, t_h_val)
            for i, (_, w_sb, w_sem) in enumerate(WLIST[:3]):
                em.wait("sync", sG, a_seed[("d2d", i)])
                dma("sync", w_sem, sb3(w_sb), bass.AP(d_wb, i * D * D, [[D, 128], [128 * D, NDS], [1, D]]))
            em.wait("vector", sV, a_seed["bc"])
            em.wait("vector", sT, t_h_val)
            va = 0
            for ds in range(NDS):
                vop(lambda e, ds=ds: e.tensor_tensor(A_xhat[:, ds * TT:(ds + 1) * TT],
                                                     A_xmcb[:, ds * TT:(ds + 1) * TT], A_abc, ALU.mult))
            for ds in range(NDS):
                va = vop(lambda e, ds=ds: e.tensor_tensor(A_xhat[:, ds * TT:(ds + 1) * TT],
                                                          A_xhat[:, ds * TT:(ds + 1) * TT], A_bbc, ALU.add),
                         inc=(ds == NDS - 1))
            a_seed["apply"] = va

        f_phase(xT, d_xmid, reload_weights=False, bar=barrier,
                pre0=dict(xcb=v_xcb0, bc=bc0, apply=v_ap0),
                tail_hook_a=a_tail_hook_a, tail_hook_b=a_tail_hook_b, mid_hook=a_mid_hook)
        barrier()

        v_bvb = v_bvb0
        v_xmcb = {0: a_seed["xmcb"]}
        em.wait("sync", sG, a_seed[("d2d", 3)])
        dma("sync", sWo, sb3(A_wo), bass.AP(d_wb, 3 * D * D, [[D, 128], [128 * D, NDS], [1, D]]))

        dma("sync", sX, sb3(EXPT_sb, 4), bass.AP(EXPT, 0, [[TT, 128], [128 * TT, 4], [1, TT]]))
        dma("sync", sX, sb3(LT_sb, 4), bass.AP(LT01, 0, [[TT, 128], [128 * TT, 4], [1, TT]]))
        dma("sync", sX, A_eye, EYE.ap())
        v_acons = sX.v
        dma("gpsimd", sB, A_decbc, d_rows.ap()[2:3, :].partition_broadcast(128))
        v_bc2 = sB.v
        for tb in range(4):
            v_bc2 = dma("sync", sB, A_deckd[:, tb:tb + 1], bass.AP(d_rows, 3 * TT + tb * 128, [[1, 128], [1, 1]]))
        def emit_mask_and_zero():
            em.wait("vector", sX, v_acons)
            v_me = vop(lambda e: e.tensor_scalar(EXPT_sb, EXPT_sb, small[:, C_LOGD:C_LOGD + 1], None,
                                                 ALU.mult), inc=True)
            em.wait("scalar", sV, v_me)
            s_mex = sop(lambda e: e.activation(EXPT_sb, EXPT_sb, AF.Exp), inc=True)
            em.wait("vector", sS, s_mex)
            vop(lambda e: e.tensor_tensor(A_maskT, EXPT_sb, LT_sb, ALU.mult), inc=True)
            vop(lambda e: e.memset(A_M, 0.0))
            return vop(lambda e: e.memset(A_Mb, 0.0), inc=True)

        v_apply2 = {-1: 0, 0: a_seed["apply"]}
        t_qkv_last = {-1: 0}
        b_bc2 = {0: a_seed["bc"]}
        t_stats2 = {}
        bank_free = [0, 0, 0, 0, 0]
        s_mb_last = 0
        v_kd_prev = 0
        w_qdt_prev = 0

        def slot_at(bj, bi):
            return bi * (bi + 1) // 2 + bj

        def a_prep_cast(sc):
            em.wait("gpsimd", sV, v_apply2[sc - 1])
            v_xmcb[sc] = dma("gpsimd", sG, sb3(A_xmcb), dram_tile(d_xmid, sc * TT))

        def a_prep_stats_v(sc):
            em.wait("vector", sG, v_xmcb[sc])
            v_sq = 0
            for q in range(4):
                v_sq = vop(lambda e, q=q: e.tensor_tensor(A_xsq[:, q * 1024:(q + 1) * 1024],
                                                          A_xmcb[:, q * 1024:(q + 1) * 1024],
                                                          A_xmcb[:, q * 1024:(q + 1) * 1024], ALU.mult),
                           inc=(q == 3))
            return ("V", v_sq)

        def a_prep_stats_t(sc, v_sq):
            em.wait("tensor", None, v_sq)
            for ds in range(NDS):
                mm(pS1[:, :], ones_bf[:, 0:1], A_xmcb[:, ds * TT:(ds + 1) * TT], start=(ds == 0),
                   stop=(ds == NDS - 1))
            ts = 0
            for ds in range(NDS):
                ts = mm(pS2[:, :], ones_bf[:, 0:1], A_xsq[:, ds * TT:(ds + 1) * TT], start=(ds == 0),
                        stop=(ds == NDS - 1), inc=(ds == NDS - 1))
            t_stats2[sc] = ts

        def a_prep_apply(sc):
            em.wait("vector", sV, b_bc2[sc])
            em.wait("vector", sT, t_qkv_last[sc - 1])
            va = 0
            for ds in range(NDS):
                vop(lambda e, ds=ds: e.tensor_tensor(A_xhat[:, ds * TT:(ds + 1) * TT],
                                                     A_xmcb[:, ds * TT:(ds + 1) * TT], A_abc, ALU.mult))
            for ds in range(NDS):
                va = vop(lambda e, ds=ds: e.tensor_tensor(A_xhat[:, ds * TT:(ds + 1) * TT],
                                                          A_xhat[:, ds * TT:(ds + 1) * TT], A_bbc, ALU.add),
                         inc=(ds == NDS - 1))
            v_apply2[sc] = va

        for sc in range(NT):
            em.wait("tensor", sV, v_apply2[sc])
            em.wait("tensor", sWq, 16)
            for do in range(NDS):
                bi = do % 5
                em.wait("tensor", sV, bank_free[bi])
                t_q = 0
                for ds in range(NDS):
                    t_q = mm(pbk[bi][:, :], A_wq[:, ds * 1024 + do * 128:ds * 1024 + (do + 1) * 128],
                             A_xhat[:, ds * TT:(ds + 1) * TT], start=(ds == 0), stop=(ds == NDS - 1),
                             inc=(ds == NDS - 1))
                em.wait("vector", sT, t_q)
                bank_free[bi] = vop(lambda e, do=do, bi=bi: e.tensor_scalar(
                    A_qT[:, do * TT:(do + 1) * TT], pbk[bi][:, :], A_bq[:, do:do + 1], None, ALU.add), inc=True)
            em.wait("vector", sW, w_qdt_prev)
            em.wait("vector", sB, v_bc2)
            v_qd = 0
            for do in range(NDS):
                v_qd = vop(lambda e, do=do: e.tensor_tensor(A_qdT[:, do * TT:(do + 1) * TT],
                                                            A_qT[:, do * TT:(do + 1) * TT], A_decbc, ALU.mult),
                           inc=(do == NDS - 1))
            em.wait("sync", sV, v_qd)
            w_qdt_prev = dma("sync", sW, dram_tile(d_qdt, sc * TT), sb3(A_qdT))

            em.wait("tensor", sWk, 16)
            for do in range(NDS):
                bi = do % 5
                em.wait("tensor", sV, bank_free[bi])
                t_k = 0
                for ds in range(NDS):
                    t_k = mm(pbk[bi][:, :], A_wk[:, ds * 1024 + do * 128:ds * 1024 + (do + 1) * 128],
                             A_xhat[:, ds * TT:(ds + 1) * TT], start=(ds == 0), stop=(ds == NDS - 1),
                             inc=(ds == NDS - 1))
                em.wait("vector", sT, t_k)
                vop(lambda e, bi=bi, do=do: e.tensor_scalar(A_tmp, pbk[bi][:, :], A_bk[:, do:do + 1], 0.0,
                                                            ALU.add, ALU.min))
                v_k1 = vop(lambda e, bi=bi, do=do: e.tensor_scalar(A_tmp2, pbk[bi][:, :], A_bk[:, do:do + 1],
                                                                   0.0, ALU.add, ALU.max), inc=True)
                bank_free[bi] = v_k1
                em.wait("scalar", sV, v_k1)
                s_e = sop(lambda e: e.activation(A_tmp, A_tmp, AF.Exp), inc=True)
                em.wait("vector", sS, s_e)
                v_kT_last = vop(lambda e, do=do: e.tensor_tensor(A_kT[:, do * TT:(do + 1) * TT], A_tmp2, A_tmp,
                                                                 ALU.add), inc=True)

            if sc == 0:
                v_m0 = emit_mask_and_zero()
            if sc + 1 < NT:
                a_prep_cast(sc + 1)
                v_sq = a_prep_stats_v(sc + 1)
                a_prep_stats_t(sc + 1, v_sq)
                v_r = ln_rows(t_stats2[sc + 1])
                b_bc2[sc + 1] = ln_bcast(v_r, A_abc, A_bbc)

            em.wait("tensor", sWv, 16)
            em.wait("tensor", sG, v_bvb)
            for tb in range(4):
                for eh in range(2):
                    bi = 2 + (tb * 2 + eh) % 3
                    em.wait("tensor", sV, bank_free[bi])
                    for ds in range(NDS):
                        mm(pbk[bi][:, :], A_xhat[:, ds * TT + tb * 128:ds * TT + tb * 128 + 128],
                           A_wv[:, ds * 1024 + eh * 512:ds * 1024 + (eh + 1) * 512],
                           start=(ds == 0), stop=False)
                    t_v = mm(pbk[bi][:, :], ones_bf[0:1, 0:128], bvb_row[0:1, eh * 512:(eh + 1) * 512],
                             start=False, stop=True, inc=True)
                    em.wait("vector", sT, t_v)
                    bank_free[bi] = vop(lambda e, tb=tb, eh=eh, bi=bi: e.tensor_copy(
                        A_v[:, tb * 1024 + eh * 512:tb * 1024 + (eh + 1) * 512], pbk[bi][:, :]), inc=True)
            t_qkv_last[sc] = sT.v

            if sc + 1 < NT:
                a_prep_apply(sc + 1)


            def do_update(skip_mb):
                nonlocal s_mb_last
                for ds in range(NDS):
                    for eh in range(2):
                        pk = [0, 1, 4][(ds * 2 + eh) % 3]
                        em.wait("tensor", sV, bank_free[pk])
                        t_u = 0
                        for tb in range(4):
                            t_u = mm(pbk[pk][:, :], A_kd[:, tb * 1024 + ds * 128:tb * 1024 + (ds + 1) * 128],
                                     A_v[:, tb * 1024 + eh * 512:tb * 1024 + (eh + 1) * 512],
                                     start=(tb == 0), stop=(tb == 3), inc=(tb == 3))
                        em.wait("vector", sT, t_u)
                        v_u = vop(lambda e, ds=ds, eh=eh, pk=pk: e.scalar_tensor_tensor(
                            A_M[:, ds * 1024 + eh * 512:ds * 1024 + (eh + 1) * 512],
                            A_M[:, ds * 1024 + eh * 512:ds * 1024 + (eh + 1) * 512],
                            small[:, C_DECC:C_DECC + 1], pbk[pk][:, :], ALU.mult, ALU.add), inc=True)
                        bank_free[pk] = v_u
                        if not skip_mb:
                            em.wait("scalar", sV, v_u)
                            s_mb_last = sop(lambda e, ds=ds, eh=eh: e.copy(
                                A_Mb[:, ds * 1024 + eh * 512:ds * 1024 + (eh + 1) * 512],
                                A_M[:, ds * 1024 + eh * 512:ds * 1024 + (eh + 1) * 512]), inc=True)
                return sV.v

            em.wait("tensor", sV, v_kT_last)
            for bi_b in range(4):
                tb = bi_b
                em.wait("tensor", sV, v_kd_prev)
                t_tr = 0
                for ds in range(NDS):
                    t_tr = tr(pTr[:, ds * 128:(ds + 1) * 128],
                              A_kT[:, ds * TT + tb * 128:ds * TT + tb * 128 + 128], inc=(ds == NDS - 1))
                em.wait("vector", sT, t_tr)
                for half in range(2):
                    v_kd_prev = vop(lambda e, tb=tb, half=half: e.tensor_scalar(
                        A_kd[:, tb * 1024 + half * 512:tb * 1024 + (half + 1) * 512],
                        pTr[:, half * 512:(half + 1) * 512], A_deckd[:, tb:tb + 1], None, ALU.mult),
                        inc=(half == 1))
                pk = bi_b % 2
                em.wait("tensor", sV, bank_free[pk])
                t_sc = 0
                for bj in range(bi_b + 1):
                    for ds in range(NDS):
                        t_sc = mm(pbk[pk][:, bj * 128:(bj + 1) * 128],
                                  A_kT[:, ds * TT + bj * 128:ds * TT + bj * 128 + 128],
                                  A_qT[:, ds * TT + bi_b * 128:ds * TT + bi_b * 128 + 128],
                                  start=(bj == 0 and ds == 0), stop=(bj == bi_b and ds == NDS - 1),
                                  inc=(bj == bi_b and ds == NDS - 1))
                em.wait("vector", sT, t_sc)
                v_at = 0
                for bj in range(bi_b + 1):
                    v_at = vop(lambda e, bi_b=bi_b, bj=bj, pk=pk: e.tensor_tensor(
                        A_at[:, slot_at(bj, bi_b) * 128:(slot_at(bj, bi_b) + 1) * 128],
                        pbk[pk][:, bj * 128:(bj + 1) * 128],
                        A_maskT[:, bj * TT + bi_b * 128:bj * TT + bi_b * 128 + 128], ALU.mult),
                        inc=(bj == bi_b))
                bank_free[pk] = v_at

            if sc == NT - 1:
                v_u_last = do_update(skip_mb=True)
                em.wait("gpsimd", sV, v_u_last)
                cc_st = dma("gpsimd", sG, bass.AP(cc_in, 0, [[D, 128], [128 * D, NDS], [1, D]]), sb3(A_M))
                em.wait("gpsimd", sG, cc_st)
                sC.v += 1
                em.on("gpsimd", lambda e, s=sC.h: e.collective_compute(
                    "AllGather", ALU.bypass, replica_groups=[[0, 1], [2, 3], [4, 5], [6, 7]],
                    ins=[cc_in.ap()], outs=[cc_out.ap()]).then_inc(s, 1))

            v_at_done = max(bank_free[0], bank_free[1])
            em.wait("tensor", sV, v_at_done)
            for es in range(NDS):
                pk = 2 + es % 3
                em.wait("tensor", sV, bank_free[pk])
                fresh = True
                if sc > 0:
                    em.wait("tensor", sS, s_mb_last)
                    for ds in range(NDS):
                        mm(pbk[pk][:, :], A_Mb[:, ds * 1024 + es * 128:ds * 1024 + (es + 1) * 128],
                           A_qdT[:, ds * TT:(ds + 1) * TT], start=fresh, stop=False)
                        fresh = False
                t_rd = 0
                for bi_b in range(4):
                    for bj in range(bi_b + 1):
                        t_rd = mm(pbk[pk][:, bi_b * 128:(bi_b + 1) * 128],
                                  A_v[:, bj * 1024 + es * 128:bj * 1024 + (es + 1) * 128],
                                  A_at[:, slot_at(bj, bi_b) * 128:(slot_at(bj, bi_b) + 1) * 128],
                                  start=fresh, stop=(bi_b == 3 and bj == bi_b),
                                  inc=(bi_b == 3 and bj == bi_b))
                        fresh = False
                rs = es % 2
                rslot = A_rstg[:, rs * 512:(rs + 1) * 512]
                qeng = "sync" if rs == 0 else "scalar"
                em.wait("vector", sT, t_rd)
                em.wait("vector", sRd[rs], sRd[rs].v)
                v_r2 = vop(lambda e, sl=rslot, pk=pk: e.tensor_copy(sl, pbk[pk][:, :]), inc=True)
                bank_free[pk] = v_r2
                em.wait(qeng, sV, v_r2)
                dma(qeng, sRd[rs], bass.AP(d_read, es * 128 * T + sc * TT, [[T, 128], [1, TT]]), rslot)

            if sc < NT - 1:
                do_update(skip_mb=False)

        barrier()

        for ds in range(NDS):
            dma("sync", sX, F_s1T[:, ds * H:(ds + 1) * H], bass.AP(d_s1t, ds * 128 * H, [[H, 128], [1, H]]))
        em.wait("sync", sC, 1)
        v_m0ld = dma("sync", sX, sb3(K_m0), bass.AP(cc_out, 0, [[D, 128], [128 * D, NDS], [1, D]]))

        def k_load(sc):
            p = sc % 2
            dma("sync", sK[p], sb3(K_qdt[p]), dram_tile(d_qdt, sc * TT))
            dma("sync", sK[p], sb3(K_rtg[p]), dram_tile(d_read, sc * TT))
            return sK[p].v

        v_xm = {0: dma("sync", sKx, sb3(K_xm[0]), dram_tile(d_xmid, 0))}
        k_ld = {0: k_load(0), 1: k_load(1)}
        bank_free = [0, 0, 0, 0, 0]
        v_corr_done = {}
        w_xp = {}
        for sc in range(NT):
            p = sc % 2
            em.wait("tensor", sK[p], k_ld[sc])
            em.wait("tensor", sX, v_m0ld)
            for es in range(NDS):
                pk = es % 4
                em.wait("tensor", sV, bank_free[pk])
                t_c = 0
                for ds in range(NDS):
                    t_c = mm(pbk[pk][:, :], K_m0[:, ds * 1024 + es * 128:ds * 1024 + (es + 1) * 128],
                             K_qdt[p][:, ds * TT:(ds + 1) * TT], start=(ds == 0), stop=(ds == NDS - 1),
                             inc=(ds == NDS - 1))
                em.wait("vector", sT, t_c)
                bank_free[pk] = vop(lambda e, es=es, pk=pk, sc=sc, p=p: e.scalar_tensor_tensor(
                    K_rfull[:, es * TT:(es + 1) * TT], pbk[pk][:, :], small[:, C_SCV + sc:C_SCV + sc + 1],
                    K_rtg[p][:, es * TT:(es + 1) * TT], ALU.mult, ALU.add), inc=True)
            v_rfull = sV.v
            em.wait("tensor", sV, v_rfull)
            em.wait("tensor", sWo, 16)
            for do in range(NDS):
                pk = do % 4
                em.wait("tensor", sV, bank_free[pk])
                t_o = 0
                for es in range(NDS):
                    t_o = mm(pbk[pk][:, :], K_wo[:, es * 1024 + do * 128:es * 1024 + (do + 1) * 128],
                             K_rfull[:, es * TT:(es + 1) * TT], start=(es == 0), stop=(es == NDS - 1),
                             inc=(es == NDS - 1))
                em.wait("vector", sT, t_o)
                if do == 0:
                    em.wait("vector", sKx, v_xm[sc])
                vop(lambda e, pk=pk, do=do: e.tensor_scalar(K_tmp, pbk[pk][:, :], A_bo[:, do:do + 1], None,
                                                            ALU.add))
                bank_free[pk] = vop(lambda e, do=do: e.tensor_tensor(
                    K_xm[0][:, do * TT:(do + 1) * TT], K_xm[0][:, do * TT:(do + 1) * TT], K_tmp, ALU.add),
                    inc=True)
            v_corr_done[sc] = sV.v
            em.wait("sync", sV, v_corr_done[sc])
            w_xp[sc] = dma("sync", sW, dram_tile(d_xpost, sc * TT), sb3(K_xm[0]))
            if sc + 2 < NT:
                k_ld[sc + 2] = k_load(sc + 2)
            if sc + 1 < NT:
                em.wait("sync", sW, w_xp[sc])
                v_xm[sc + 1] = dma("sync", sKx, sb3(K_xm[0]), dram_tile(d_xmid, (sc + 1) * TT))
            if sc == 0:
                em.wait("gpsimd", sW, w_xp[0])
                f2_xcb = dma("gpsimd", sG, sb3(F_xcb), dram_tile(d_xpost, 0))
                em.wait("scalar", sG, f2_xcb)
                s_sqf = 0
                for q in range(4):
                    s_sqf = sop(lambda e, q=q: e.square(F_xsq[:, q * 1024:(q + 1) * 1024],
                                                        F_xcb[:, q * 1024:(q + 1) * 1024]), inc=(q == 3))
                em.wait("tensor", sS, s_sqf)
                for ds in range(NDS):
                    mm(pS1[:, :], ones_bf[:, 0:1], F_xcb[:, ds * TT:(ds + 1) * TT], start=(ds == 0),
                       stop=(ds == NDS - 1))
                tsf = 0
                for ds in range(NDS):
                    tsf = mm(pS2[:, :], ones_bf[:, 0:1], F_xsq[:, ds * TT:(ds + 1) * TT], start=(ds == 0),
                             stop=(ds == NDS - 1), inc=(ds == NDS - 1))
                v_rf = ln_rows(tsf)
                f2_abc = F_ring[:, 0:512]
                f2_bbc = F_ring[:, 512:1024]
                f2_bc = ln_bcast(v_rf, f2_abc, f2_bbc)
                em.wait("vector", sV, f2_bc)
                v_apf = 0
                for ds in range(NDS):
                    vop(lambda e, ds=ds, a=f2_abc: e.tensor_tensor(F_xhat[:, ds * TT:(ds + 1) * TT],
                                                                   F_xcb[:, ds * TT:(ds + 1) * TT], a, ALU.mult))
                for ds in range(NDS):
                    v_apf = vop(lambda e, ds=ds, b=f2_bbc: e.tensor_tensor(F_xhat[:, ds * TT:(ds + 1) * TT],
                                                                           F_xhat[:, ds * TT:(ds + 1) * TT], b,
                                                                           ALU.add), inc=(ds == NDS - 1))

        f_phase(d_xpost, outT, reload_weights=True, bar=barrier,
                pre0=dict(xcb=f2_xcb, bc=f2_bc, apply=v_apf))

        for eng in ENGS:
            for dsem in ALL_DMA:
                em.wait(eng, dsem, dsem.v)

        with nc.Block() as block:
            em.run(block)

    return nc


def kernel(**inputs):
    x = np.asarray(inputs["L0_attn_x"], np.float32)
    B, S, _ = x.shape
    assert (B, S) == (4, 4096)

    w1 = np.asarray(inputs["L0_ffn_w1"], np.float32)
    w2 = np.asarray(inputs["L0_ffn_w2"], np.float32)

    com = dict(
        wq=np.ascontiguousarray(np.asarray(inputs["L0_attn_Wq"], np.float32)),
        wk=np.ascontiguousarray(np.asarray(inputs["L0_attn_Wk"], np.float32)),
        wv=np.ascontiguousarray(np.asarray(inputs["L0_attn_Wv"], np.float32)),
        wo=np.ascontiguousarray(np.asarray(inputs["L0_attn_Wo"], np.float32)),
        bq=np.asarray(inputs["L0_attn_bq"], np.float32),
        bk=np.asarray(inputs["L0_attn_bk"], np.float32),
        bv=np.asarray(inputs["L0_attn_bv"], np.float32),
        bo=np.asarray(inputs["L0_attn_bo"], np.float32),
        w1t=np.ascontiguousarray(w1.T),
        w2t=np.ascontiguousarray(w2.T),
        b1=np.asarray(inputs["L0_ffn_b1"], np.float32),
        b2=np.asarray(inputs["L0_ffn_b2"], np.float32),
        expj=np.arange(TT, dtype=np.float32),
        ekd=(TT - 1 - np.arange(TT)).astype(np.float32),
        eye=np.eye(128).astype(BF),
    )
    jj = np.arange(TT)
    lt = (jj[:, None] < jj[None, :]).astype(np.float32)
    e_mat = (jj[None, :] - 1 - jj[:, None]).astype(np.float32)
    com["expt"] = np.where(lt > 0, e_mat, 0.0).astype(np.float32)
    com["lt01"] = lt

    dp = float(np.asarray(inputs["L0_attn_decay_param"], np.float32).reshape(()))

    in_maps = []
    for c in range(8):
        b, h = c // 2, c % 2
        im = dict(com)
        im["xT"] = np.ascontiguousarray(x[b, h * T:(h + 1) * T, :].T)
        im["dpar"] = np.asarray([dp, float(h)], np.float32)
        im["poff"] = np.asarray([(c ^ 1) * D * D], np.int32)
        in_maps.append(im)

    nc = build(dbg=DBG)
    res = run_bass_kernel_spmd(nc, in_maps, core_ids=list(range(8)))

    out = np.empty((B, S, D), np.float32)
    for c in range(8):
        b, h = c // 2, c % 2
        out[b, h * T:(h + 1) * T, :] = np.asarray(res.results[c]["outT"], np.float32).T
    if DBG:
        kernel.dbg = res.results
    return out


# version 15
# speedup vs baseline: 1.1042x; 1.1042x over previous
import numpy as np
import ml_dtypes
import concourse.bass as bass
import concourse.mybir as mybir
from concourse.bass_utils import run_bass_kernel_spmd
from contextlib import ExitStack

F32 = mybir.dt.float32
BF16 = mybir.dt.bfloat16
AF = mybir.ActivationFunctionType
ALU = mybir.AluOpType
BF = ml_dtypes.bfloat16

D, H, T, TT = 1024, 4096, 2048, 512
NT, NDS, NHS = T // TT, D // 128, H // 128
EPS = 1e-5
ENGS = ("sync", "gpsimd", "tensor", "scalar", "vector")
DBG = False


class Sem:
    def __init__(self, nc, name):
        self.h = nc.alloc_semaphore(name=name)
        self.v = 0


class Em:
    def __init__(self):
        self.streams = {e: [] for e in ENGS}

    def on(self, eng, fn):
        self.streams[eng].append(fn)

    def wait(self, eng, sem, val):
        if isinstance(val, tuple):
            sem2, val = (val[0], val[1])
            if val > 0:
                self.on(eng, lambda e, s=self.sB.h, v=val: e.wait_ge(s, v))
            return
        if val > 0:
            self.on(eng, lambda e, s=sem.h, v=val: e.wait_ge(s, v))

    def emit(self, eng, sem, n, fn):
        if sem is None:
            self.on(eng, lambda e, f=fn: f(e))
            return None
        sem.v += n
        self.on(eng, lambda e, f=fn, s=sem.h, m=n: f(e).then_inc(s, m))
        return sem.v

    def run(self, block):
        for eng in ENGS:
            fns = self.streams[eng]
            getattr(block, eng)(lambda e, fns=fns: [f(e) for f in fns])


def build(dbg=False):
    nc = bass.Bass("TRN2", target_bir_lowering=False)

    xT = nc.dram_tensor("xT", [D, T], F32, kind="ExternalInput")
    WQ = nc.dram_tensor("wq", [D, D], F32, kind="ExternalInput")
    WK = nc.dram_tensor("wk", [D, D], F32, kind="ExternalInput")
    WV = nc.dram_tensor("wv", [D, D], F32, kind="ExternalInput")
    WO = nc.dram_tensor("wo", [D, D], F32, kind="ExternalInput")
    BQ = nc.dram_tensor("bq", [D], F32, kind="ExternalInput")
    BK = nc.dram_tensor("bk", [D], F32, kind="ExternalInput")
    BV = nc.dram_tensor("bv", [D], F32, kind="ExternalInput")
    BO = nc.dram_tensor("bo", [D], F32, kind="ExternalInput")
    W1T = nc.dram_tensor("w1t", [D, H], F32, kind="ExternalInput")
    W2T = nc.dram_tensor("w2t", [H, D], F32, kind="ExternalInput")
    B1 = nc.dram_tensor("b1", [H], F32, kind="ExternalInput")
    B2 = nc.dram_tensor("b2", [D], F32, kind="ExternalInput")
    DPAR = nc.dram_tensor("dpar", [2], F32, kind="ExternalInput")
    EXPJ = nc.dram_tensor("expj", [TT], F32, kind="ExternalInput")
    EKD = nc.dram_tensor("ekd", [TT], F32, kind="ExternalInput")
    EXPT = nc.dram_tensor("expt", [TT, TT], F32, kind="ExternalInput")
    LT01 = nc.dram_tensor("lt01", [TT, TT], F32, kind="ExternalInput")
    EYE = nc.dram_tensor("eye", [128, 128], BF16, kind="ExternalInput")

    outT = nc.dram_tensor("outT", [D, T], F32, kind="ExternalOutput")

    sk = "ExternalOutput" if dbg else "Internal"
    d_xmid = nc.dram_tensor("d_xmid", [D, T], F32, kind=sk)
    d_xpost = nc.dram_tensor("d_xpost", [D, T], F32, kind=sk)
    d_read = nc.dram_tensor("d_read", [D, T], F32, kind=sk)
    d_qdt = nc.dram_tensor("d_qdt", [D, T], BF16, kind=sk)
    d_s1t = nc.dram_tensor("d_s1t", [D, H], BF16)
    d_s2t = nc.dram_tensor("d_s2t", [H, D], BF16)
    d_wb = nc.dram_tensor("d_wb", [4 * D, D], BF16)
    d_rows = nc.dram_tensor("d_rows", [4, TT], F32)
    d_small = nc.dram_tensor("d_small", [16], F32)
    cc_in = nc.dram_tensor("cc_in", [D, D], BF16)
    cc_out = nc.dram_tensor("cc_out", [2 * D, D], BF16)
    POFF = nc.dram_tensor("poff", [1], mybir.dt.int32, kind="ExternalInput")

    with ExitStack() as ctx:
        E = ctx.enter_context
        AW = 52750
        arena = E(nc.sbuf_tensor("arena", [128, AW], F32))
        poff_t = E(nc.sbuf_tensor("poff_t", [1, 1], mybir.dt.int32))
        ar = arena.ap()

        def reg(off, words, dtype=F32):
            assert off + words <= AW, (off, words)
            a = ar[:, off:off + words]
            return a.bitcast(BF16) if dtype == BF16 else a

        ones_bf = reg(0, 64, BF16)
        alpha1g = reg(64, 32)
        b1t = reg(96, 32)
        alpha2h = reg(128, 8)
        b2h = reg(136, 8)
        small = reg(144, 16)
        rows_r = reg(160, 1536)
        P_bq = reg(1696, 8)
        P_bk = reg(1704, 8)
        P_bo = reg(1712, 8)
        PB = 1720
        C_LOGD, C_FLAG, C_DECC, C_SCV = 0, 1, 2, 4
        sl0 = rows_r[0:1, 0:TT]
        bchl = rows_r[0:1, 0:512].bitcast(BF16)
        sl1 = rows_r[0:1, TT:2 * TT]
        sl2 = rows_r[0:1, 2 * TT:3 * TT]
        r_dec, r_ekd = sl0, sl1
        r_dp = rows_r[0:1, 2 * TT:2 * TT + 8]
        r_bar = rows_r[0:1, 3 * TT - 1:3 * TT]
        poff_sb = poff_t[0:1, 0:1]

        F_s1T = reg(PB + 0, 16384, BF16)
        F_s2T = reg(PB + 16384, 16384, BF16)
        F_xcb = reg(PB + 32768, 2048, BF16)
        F_xsq = reg(PB + 34816, 2048, BF16)
        F_xhat = reg(PB + 36864, 2048, BF16)
        F_h = reg(PB + 38912, 8192, BF16)
        F_ring = reg(PB + 47104, 1024)
        F_abc = reg(PB + 48128, 512)
        F_bbc = reg(PB + 48640, 512)
        F_tmp = reg(PB + 49152, 512)
        wstgP = [reg(PB + 38912, 2048, BF16), reg(PB + 40960, 2048, BF16)]
        awP = [reg(PB + 43008, 2048, BF16), reg(PB + 45056, 2048, BF16)]

        A_wq = reg(PB + 0, 4096, BF16)
        A_wk = reg(PB + 4096, 4096, BF16)
        A_wv = reg(PB + 8192, 4096, BF16)
        A_qT = reg(PB + 12288, 2048, BF16)
        A_kT = reg(PB + 14336, 2048, BF16)
        A_wo = reg(PB + 16384, 4096, BF16)
        A_M = reg(PB + 20480, 8192)
        A_Mb = reg(PB + 28672, 4096, BF16)
        A_xmcb = reg(PB + 32768, 2048, BF16)
        A_xsq = reg(PB + 34816, 2048, BF16)
        A_xhat = reg(PB + 36864, 2048, BF16)
        A_kd = reg(PB + 38912, 2048, BF16)
        A_qdT = reg(PB + 40960, 2048, BF16)
        A_at = reg(PB + 43008, 640, BF16)
        A_rstg = reg(PB + 43648, 1024)
        A_v = reg(PB + 44672, 2048, BF16)
        A_tmp = reg(PB + 46720, 512)
        A_tmp2 = reg(PB + 47232, 512)
        A_deckd = reg(PB + 47744, 4)
        A_eye = reg(PB + 47748, 64, BF16)
        A_abc = F_abc
        A_bbc = F_bbc
        A_maskT = reg(PB + 49152, 1024, BF16)
        A_decbc = reg(PB + 50176, 256, BF16)
        A_bvb = reg(PB + 50432, 512, BF16)
        A_bq, A_bk, A_bo = P_bq, P_bk, P_bo
        EXPT_sb = reg(PB + 20480, 2048)
        LT_sb = reg(PB + 22528, 2048)
        bvb_row = A_bvb[0:1, 0:1024]

        K_wo = A_wo
        K_m0 = reg(PB + 20480, 4096, BF16)
        K_rtg = [reg(PB + 24576, 2048, BF16), reg(PB + 26624, 2048, BF16)]
        K_xm = [reg(PB + 38912, 4096), reg(PB + 38912, 4096)]
        K_qdt = [reg(PB + 43008, 2048, BF16), reg(PB + 45056, 2048, BF16)]
        K_rfull = reg(PB + 48128, 2048, BF16)
        K_tmp = reg(PB + 50176, 512)

        pbk = [E(nc.psum_tensor(f"pb{i}", [128, 512], F32)) for i in range(4)]
        pTr = E(nc.psum_tensor("pTr", [128, 1024], BF16))
        pS1 = E(nc.psum_tensor("pS1", [1, 512], F32))
        pS2 = E(nc.psum_tensor("pS2", [1, 512], F32))
        pBC = E(nc.psum_tensor("pBC", [128, 512], F32))
        pbk.append(pBC)

        sT, sV, sS = Sem(nc, "sT"), Sem(nc, "sV"), Sem(nc, "sS")
        sX, sW, sB = Sem(nc, "sX"), Sem(nc, "sW"), Sem(nc, "sB")
        sG, sC = Sem(nc, "sG"), Sem(nc, "sC")
        sWq, sWk, sWv, sWo = (Sem(nc, n) for n in ("sWq", "sWk", "sWv", "sWo"))
        sR = [Sem(nc, "sR0"), Sem(nc, "sR1")]
        sRd = [Sem(nc, "sRd0"), Sem(nc, "sRd1"), Sem(nc, "sRd2"), Sem(nc, "sRd3")]
        sK = [Sem(nc, "sKa"), Sem(nc, "sKb")]
        sKx = Sem(nc, "sKx")

        em = Em()
        em.sB = sB

        def dma(eng, sem, out_ap, in_ap, slow=False):
            if slow:
                return em.emit(eng, sem, 16,
                               lambda e, ou=out_ap, ii=in_ap: e.dma_start(out=ou, in_=ii,
                                                                          allow_slow_non_contiguous=True))
            return em.emit(eng, sem, 16, lambda e, ou=out_ap, ii=in_ap: e.dma_start(out=ou, in_=ii))

        def mm(out_ap, lhsT, rhs, start, stop, inc=False):
            return em.emit("tensor", sT if inc else None, 1,
                           lambda e, ou=out_ap, l=lhsT, r=rhs, a=start, b=stop: e.matmul(ou, l, r, start=a, stop=b))

        def tr(out_ap, in_ap, inc=False):
            return em.emit("tensor", sT if inc else None, 1,
                           lambda e, ou=out_ap, ii=in_ap: e.transpose(ou, ii, A_eye))

        def vop(fn, inc=False):
            return em.emit("vector", sV if inc else None, 1, fn)

        def sop(fn, inc=False):
            return em.emit("scalar", sS if inc else None, 1, fn)

        ALL_DMA = (sX, sW, sB, sG, sWq, sWk, sWv, sWo, sR[0], sR[1], sRd[0], sRd[1], sRd[2], sRd[3], sK[0], sK[1], sKx)

        def barrier():
            sT.v += 1
            em.on("tensor", lambda e, s=sT.h: e.matmul(pS1[0:1, 0:1], ones_bf[:, 0:1], ones_bf[:, 0:1],
                                                       start=True, stop=True).then_inc(s, 1))
            vt = sT.v
            sS.v += 1
            em.on("scalar", lambda e, s=sS.h: e.copy(r_bar, r_bar).then_inc(s, 1))
            vs = sS.v
            sV.v += 1
            em.on("vector", lambda e, s=sV.h: e.engine_nop().then_inc(s, 1))
            vv = sV.v
            for eng in ENGS:
                if eng != "tensor":
                    em.wait(eng, sT, vt)
                if eng != "scalar":
                    em.wait(eng, sS, vs)
                if eng != "vector":
                    em.wait(eng, sV, vv)
                for dsem in ALL_DMA:
                    em.wait(eng, dsem, dsem.v)

        def dram_tile(t_dram, tok0, nds=NDS, ntok=TT, cols=T):
            return bass.AP(t_dram, tok0, [[cols, 128], [128 * cols, nds], [1, ntok]])

        def sb3(sb_ap, nds=NDS):
            return sb_ap.rearrange("p (a t) -> p a t", a=nds)

        def ln_rows(t_stats):
            em.wait("vector", sT, t_stats)
            vop(lambda e: e.tensor_copy(sl0, pS1[:, :]))
            vop(lambda e: e.tensor_copy(sl1, pS2[:, :]))
            vop(lambda e: e.scalar_tensor_tensor(sl2, sl0, -1.0 / D, sl0, ALU.mult, ALU.mult))
            vop(lambda e: e.tensor_tensor(sl2, sl1, sl2, ALU.add))
            vop(lambda e: e.tensor_scalar(sl2, sl2, 1.0 / D, EPS, ALU.mult, ALU.add))
            v2 = vop(lambda e: e.reciprocal(sl2, sl2), inc=True)
            em.wait("scalar", sV, v2)
            s1 = sop(lambda e: e.activation(sl2, sl2, AF.Sqrt), inc=True)
            em.wait("vector", sS, s1)
            return vop(lambda e: e.scalar_tensor_tensor(sl1, sl0, -1.0 / D, sl2, ALU.mult, ALU.mult), inc=True)

        def ln_bcast(v_rows, abc, bbc):
            em.wait("sync", sV, v_rows)
            dma("sync", sB, d_rows.ap()[0:1, :], sl2)
            dma("sync", sB, d_rows.ap()[1:2, :], sl1)
            em.wait("sync", sB, sB.v)
            dma("sync", sB, abc, d_rows.ap()[0:1, :].partition_broadcast(128))
            bcv = dma("sync", sB, bbc, d_rows.ap()[1:2, :].partition_broadcast(128))
            sV.v += 0
            return ("B", bcv)

        v_xcb0 = dma("gpsimd", sG, sb3(F_xcb), dram_tile(xT, 0))
        dma("sync", sX, r_dp[0:1, 0:2], DPAR.ap()[None, :])
        dma("sync", sX, poff_sb, POFF.ap()[None, :])
        dma("sync", sX, r_dec, EXPJ.ap()[None, :])
        dma("sync", sX, r_ekd, EKD.ap()[None, :])
        dma("sync", sX, b1t, bass.AP(B1, 0, [[1, 128], [128, 32]]), slow=True)
        dma("sync", sX, b2h, bass.AP(B2, 0, [[1, 128], [128, 8]]), slow=True)
        for t_d, dst in ((BQ, P_bq), (BK, P_bk), (BO, P_bo)):
            dma("sync", sX, dst, bass.AP(t_d, 0, [[1, 128], [128, 8]]), slow=True)
        v_ld0 = sX.v
        v_bvb0 = dma("gpsimd", sG, bvb_row, BV.ap()[None, :])

        em.wait("vector", sX, v_ld0)
        vop(lambda e: e.memset(ones_bf, 1.0))
        vop(lambda e: e.tensor_scalar(b2h, b2h, 0.5, None, ALU.mult), inc=True)

        em.wait("scalar", sX, v_ld0)
        sop(lambda e: e.activation(r_dp[0:1, 2:3], r_dp[0:1, 0:1], AF.Sigmoid))
        v_logd = sop(lambda e: e.activation(r_dp[0:1, 3:4], r_dp[0:1, 2:3], AF.Ln), inc=True)
        em.wait("sync", sS, v_logd)
        dma("sync", sB, d_small.ap()[None, 0:1], r_dp[0:1, 3:4])
        dma("sync", sB, d_small.ap()[None, 1:2], r_dp[0:1, 1:2])
        em.wait("sync", sB, sB.v)
        dma("sync", sB, small[:, C_LOGD:C_LOGD + 1], d_small.ap()[None, 0:1].partition_broadcast(128))
        v_bc0 = dma("sync", sB, small[:, C_FLAG:C_FLAG + 1], d_small.ap()[None, 1:2].partition_broadcast(128))

        em.wait("vector", sB, v_bc0)
        vop(lambda e: e.tensor_scalar(small[:, C_DECC:C_DECC + 1], small[:, C_LOGD:C_LOGD + 1], float(TT),
                                      None, ALU.mult))
        for i in range(4):
            vop(lambda e, i=i: e.tensor_scalar(small[:, C_SCV + i:C_SCV + i + 1], small[:, C_LOGD:C_LOGD + 1],
                                               float(TT * i), None, ALU.mult))
        vop(lambda e: e.tensor_scalar(r_dec, r_dec, small[0:1, C_LOGD:C_LOGD + 1], None, ALU.mult))
        v_pre = vop(lambda e: e.tensor_scalar(r_ekd, r_ekd, small[0:1, C_LOGD:C_LOGD + 1], None, ALU.mult),
                    inc=True)
        em.wait("scalar", sV, v_pre)
        sop(lambda e: e.activation(small[:, C_DECC:C_DECC + 1], small[:, C_DECC:C_DECC + 1], AF.Exp))
        for i in range(4):
            sop(lambda e, i=i: e.activation(small[:, C_SCV + i:C_SCV + i + 1], small[:, C_SCV + i:C_SCV + i + 1],
                                            AF.Exp))
        sop(lambda e: e.activation(r_dec, r_dec, AF.Exp))
        v_exp1 = sop(lambda e: e.activation(r_ekd, r_ekd, AF.Exp), inc=True)
        em.wait("vector", sS, v_exp1)
        for i in range(4):
            vop(lambda e, i=i: e.tensor_tensor(small[:, C_SCV + i:C_SCV + i + 1],
                                               small[:, C_SCV + i:C_SCV + i + 1],
                                               small[:, C_FLAG:C_FLAG + 1], ALU.mult), inc=(i == 3))
        em.wait("sync", sS, v_exp1)
        dma("sync", sB, d_rows.ap()[2:3, :], r_dec)
        v_decb = dma("sync", sB, d_rows.ap()[3:4, :], r_ekd)

        em.wait("vector", sG, v_xcb0)
        v_sq0 = 0
        for q in range(4):
            v_sq0 = vop(lambda e, q=q: e.tensor_tensor(F_xsq[:, q * 1024:(q + 1) * 1024],
                                                       F_xcb[:, q * 1024:(q + 1) * 1024],
                                                       F_xcb[:, q * 1024:(q + 1) * 1024], ALU.mult),
                        inc=(q == 3))
        em.wait("tensor", sV, v_sq0)
        for ds in range(NDS):
            mm(pS1[:, :], ones_bf[:, 0:1], F_xcb[:, ds * TT:(ds + 1) * TT], start=(ds == 0),
               stop=(ds == NDS - 1))
        t_st0 = 0
        for ds in range(NDS):
            t_st0 = mm(pS2[:, :], ones_bf[:, 0:1], F_xsq[:, ds * TT:(ds + 1) * TT], start=(ds == 0),
                       stop=(ds == NDS - 1), inc=(ds == NDS - 1))
        em.wait("vector", sB, v_decb)
        v_r0 = ln_rows(t_st0)
        bc0 = ln_bcast(v_r0, F_abc, F_bbc)
        em.wait("vector", sV, bc0)
        v_ap0 = 0
        for ds in range(NDS):
            vop(lambda e, ds=ds: e.tensor_tensor(F_xhat[:, ds * TT:(ds + 1) * TT],
                                                 F_xcb[:, ds * TT:(ds + 1) * TT], F_abc, ALU.mult))
        for ds in range(NDS):
            v_ap0 = vop(lambda e, ds=ds: e.tensor_tensor(F_xhat[:, ds * TT:(ds + 1) * TT],
                                                         F_xhat[:, ds * TT:(ds + 1) * TT], F_bbc, ALU.add),
                        inc=(ds == NDS - 1))

        s_sign = {}
        v_aw = {}
        t_alv = {}
        first_al = [True, True]
        t_al1 = 0
        t_al2 = 0
        for ci in range(16):
            p = ci % 2
            w2g = ci >= 8
            g = ci - 8
            wst = wstgP[p]
            aw = awP[p]
            if ci >= 2:
                em.wait("gpsimd", sS, s_sign[ci - 2])
                em.wait("gpsimd", sV, v_aw[ci - 2])
            if not w2g:
                dma("gpsimd", sG, wst, bass.AP(W1T, ci * 128 * H, [[H, 128], [1, H]]))
            else:
                dma("gpsimd", sG, sb3(wst, 4), bass.AP(W2T, g * 512 * D, [[D, 128], [128 * D, 4], [1, D]]))
            v_w = sG.v
            em.wait("scalar", sG, v_w)
            if not w2g:
                s_sign[ci] = sop(lambda e, ci=ci, wst=wst: e.sign(F_s1T[:, ci * H:(ci + 1) * H], wst), inc=True)
            else:
                s_sign[ci] = sop(lambda e, g=g, wst=wst: e.sign(F_s2T[:, g * 4096:(g + 1) * 4096], wst),
                                 inc=True)
            em.wait("vector", sG, v_w)
            if w2g and g == 0:
                em.wait("vector", sT, t_al1)
                v_al1 = vop(lambda e: e.tensor_scalar(alpha1g, pbk[0][:, 0:32], 1.0 / D, None, ALU.mult), inc=True)
            v_aw[ci] = vop(lambda e, aw=aw, wst=wst: e.scalar_tensor_tensor(aw[:, 0:H], wst, -1.0, wst,
                                                                            ALU.mult, ALU.max), inc=True)
            em.wait("tensor", sV, v_aw[ci])
            if not w2g:
                for hs in range(NHS):
                    t_al1 = mm(pbk[0][:, hs:hs + 1], aw[:, hs * 128:(hs + 1) * 128], ones_bf[:, 0:1],
                               start=first_al[0], stop=(ci == 7 and hs == NHS - 1), inc=(hs == NHS - 1))
                    first_al[0] = False
            else:
                if g == 0:
                    em.wait("tensor", sV, v_al1)
                for k in range(4):
                    for ds in range(NDS):
                        t_al2 = mm(pbk[0][:, ds:ds + 1], aw[:, k * 1024 + ds * 128:k * 1024 + (ds + 1) * 128],
                                   ones_bf[:, 0:1], start=first_al[1],
                                   stop=(g == 7 and k == 3 and ds == NDS - 1), inc=(k == 3 and ds == NDS - 1))
                        first_al[1] = False
            em.wait("sync", sS, s_sign[ci])
            if not w2g:
                dma("sync", sW, bass.AP(d_s1t, ci * 128 * H, [[H, 128], [1, H]]), F_s1T[:, ci * H:(ci + 1) * H])
            else:
                dma("sync", sW, bass.AP(d_s2t, g * 512 * D, [[D, 128], [128 * D, 4], [1, D]]),
                    sb3(F_s2T[:, g * 4096:(g + 1) * 4096], 4))
        em.wait("vector", sT, t_al2)
        vop(lambda e: e.tensor_scalar(alpha2h, pbk[0][:, 0:8], 0.5 / H, None, ALU.mult), inc=True)

        def f_phase(xin, xout, reload_weights, bar=None, cast0_gate=None, pre0=None,
                    tail_hook_a=None, tail_hook_b=None, mid_hook=None):

            v_apply = {-1: 0}
            t_h_last = {-1: 0}
            v_xcb = {}
            b_bc = {}
            t_stats = {}
            bank_free_v = [0, 0, 0, 0, 0]
            bank_free_s = [0, 0, 0, 0, 0]
            slot_store = [sG.v, sG.v]

            def prep_cast(t):
                em.wait("gpsimd", sV, v_apply[t - 1])
                if t == 0 and cast0_gate is not None:
                    em.wait("gpsimd", cast0_gate[0], cast0_gate[1])
                v_xcb[t] = dma("gpsimd", sG, sb3(F_xcb), dram_tile(xin, t * TT))

            def prep_stats_v(t):
                em.wait("vector", sG, v_xcb[t])
                v_sq = 0
                for q in range(4):
                    v_sq = vop(lambda e, q=q: e.tensor_tensor(F_xsq[:, q * 1024:(q + 1) * 1024],
                                                              F_xcb[:, q * 1024:(q + 1) * 1024],
                                                              F_xcb[:, q * 1024:(q + 1) * 1024], ALU.mult),
                               inc=(q == 3))
                return v_sq

            def prep_stats_t(t, v_sq):
                em.wait("tensor", sV, v_sq)
                for ds in range(NDS):
                    mm(pS1[:, :], ones_bf[:, 0:1], F_xcb[:, ds * TT:(ds + 1) * TT], start=(ds == 0),
                       stop=(ds == NDS - 1))
                ts = 0
                for ds in range(NDS):
                    ts = mm(pS2[:, :], ones_bf[:, 0:1], F_xsq[:, ds * TT:(ds + 1) * TT], start=(ds == 0),
                            stop=(ds == NDS - 1), inc=(ds == NDS - 1))
                t_stats[t] = ts

            def prep_apply(t):
                em.wait("vector", sV, b_bc[t])
                em.wait("vector", sT, t_h_last[t - 1])
                va = 0
                for ds in range(NDS):
                    vop(lambda e, ds=ds: e.tensor_tensor(F_xhat[:, ds * TT:(ds + 1) * TT],
                                                         F_xcb[:, ds * TT:(ds + 1) * TT], F_abc, ALU.mult))
                for ds in range(NDS):
                    va = vop(lambda e, ds=ds: e.tensor_tensor(F_xhat[:, ds * TT:(ds + 1) * TT],
                                                              F_xhat[:, ds * TT:(ds + 1) * TT], F_bbc, ALU.add),
                             inc=(ds == NDS - 1))
                v_apply[t] = va

            if pre0 is None:
                prep_cast(0)
                v_sq0 = prep_stats_v(0)
                prep_stats_t(0, v_sq0)
                v_r0 = ln_rows(t_stats[0])
                b_bc[0] = ln_bcast(v_r0, F_abc, F_bbc)
                prep_apply(0)
            else:
                v_xcb[0] = pre0["xcb"]
                b_bc[0] = pre0["bc"]
                v_apply[0] = pre0["apply"]
            if bar is not None:
                bar()
            if reload_weights:
                for g in range(8):
                    dma("sync", sX, sb3(F_s2T[:, g * 4096:(g + 1) * 4096], 4),
                        bass.AP(d_s2t, g * 512 * D, [[D, 128], [128 * D, 4], [1, D]]))
            v_s2_ld = sX.v

            for t in range(NT):
                em.wait("tensor", sV, v_apply[t])
                for hs in range(NHS):
                    bi = hs % 5
                    em.wait("tensor", sS, bank_free_s[bi])
                    em.wait("tensor", sV, bank_free_v[bi])
                    t_h = 0
                    for ds in range(NDS):
                        t_h = mm(pbk[bi][:, :], F_s1T[:, ds * H + hs * 128:ds * H + (hs + 1) * 128],
                                 F_xhat[:, ds * TT:(ds + 1) * TT], start=(ds == 0), stop=(ds == NDS - 1),
                                 inc=(ds == NDS - 1))
                    em.wait("scalar", sT, t_h)
                    bank_free_s[bi] = sop(lambda e, hs=hs, bi=bi: e.activation(
                        F_h[:, hs * TT:(hs + 1) * TT], pbk[bi][:, :], AF.Gelu,
                        bias=b1t[:, hs:hs + 1], scale=alpha1g[:, hs:hs + 1]), inc=True)
                    if hs == 7:
                        if t + 1 < NT:
                            prep_cast(t + 1)
                            v_sq = prep_stats_v(t + 1)
                            prep_stats_t(t + 1, v_sq)
                            v_r = ln_rows(t_stats[t + 1])
                            b_bc[t + 1] = ln_bcast(v_r, F_abc, F_bbc)
                        elif tail_hook_a is not None:
                            tail_hook_a(v_apply[t])
                s_gelu_all = sS.v
                t_h_last[t] = sT.v

                for ds2 in range(2):
                    em.wait("sync", sG, slot_store[ds2 % 2])
                    dma("sync", sR[ds2 % 2], F_ring[:, (ds2 % 2) * 512:(ds2 % 2 + 1) * 512],
                        bass.AP(xin, ds2 * 128 * T + t * TT, [[T, 128], [1, TT]]))

                if reload_weights and t == 0:
                    em.wait("tensor", sX, v_s2_ld)
                em.wait("tensor", sS, s_gelu_all)
                for ds2 in range(NDS):
                    bi = ds2 % 5
                    em.wait("tensor", sS, bank_free_s[bi])
                    em.wait("tensor", sV, bank_free_v[bi])
                    t_o2 = 0
                    for hs in range(NHS):
                        t_o2 = mm(pbk[bi][:, :], F_s2T[:, hs * 1024 + ds2 * 128:hs * 1024 + (ds2 + 1) * 128],
                                  F_h[:, hs * TT:(hs + 1) * TT], start=(hs == 0), stop=(hs == NHS - 1),
                                  inc=(hs == NHS - 1))
                    sl = ds2 % 2
                    em.wait("vector", sT, t_o2)
                    em.wait("vector", sR[sl], sR[sl].v)
                    vop(lambda e, bi=bi, ds2=ds2: e.tensor_scalar(F_tmp, pbk[bi][:, :], alpha2h[:, ds2:ds2 + 1],
                                                                  b2h[:, ds2:ds2 + 1], ALU.mult, ALU.add))
                    v_ev = vop(lambda e, sl=sl: e.tensor_tensor(F_ring[:, sl * 512:(sl + 1) * 512],
                                                                F_ring[:, sl * 512:(sl + 1) * 512], F_tmp,
                                                                ALU.add), inc=True)
                    bank_free_v[bi] = v_ev
                    em.wait("gpsimd", sV, v_ev)
                    em.wait("gpsimd", sG, sG.v)
                    slot_store[sl] = dma("gpsimd", sG, bass.AP(xout, ds2 * 128 * T + t * TT, [[T, 128], [1, TT]]),
                                         F_ring[:, sl * 512:(sl + 1) * 512])
                    if ds2 + 2 < NDS:
                        em.wait("sync", sG, slot_store[sl])
                        dma("sync", sR[sl], F_ring[:, sl * 512:(sl + 1) * 512],
                            bass.AP(xin, (ds2 + 2) * 128 * T + t * TT, [[T, 128], [1, TT]]))
                    if ds2 == 1:
                        if t + 1 < NT:
                            prep_apply(t + 1)
                        elif tail_hook_b is not None:
                            tail_hook_b(t_h_last[t])
                    if ds2 == NDS - 1 and mid_hook is not None:
                        mid_hook(t)

        a_seed = {}
        WLIST = ((WQ, A_wq, sWq), (WK, A_wk, sWk), (WV, A_wv, sWv), (WO, A_wo, sWo))

        def w_d2d(i):
            a_seed[("d2d", i)] = dma("gpsimd", sG, bass.AP(d_wb, i * D * D, [[D, 128], [128 * D, NDS], [1, D]]),
                                     bass.AP(WLIST[i][0], 0, [[D, 128], [128 * D, NDS], [1, D]]))

        def a_mid_hook(t):
            if t < 3:
                w_d2d(t)

        def a_tail_hook_a(v_apply_last):
            w_d2d(3)
            em.wait("gpsimd", sV, v_apply_last)
            a_seed["xmcb"] = dma("gpsimd", sG, sb3(A_xmcb), dram_tile(d_xmid, 0))
            em.wait("vector", sG, a_seed["xmcb"])
            v_sq = 0
            for q in range(4):
                v_sq = vop(lambda e, q=q: e.tensor_tensor(A_xsq[:, q * 1024:(q + 1) * 1024],
                                                          A_xmcb[:, q * 1024:(q + 1) * 1024],
                                                          A_xmcb[:, q * 1024:(q + 1) * 1024], ALU.mult),
                           inc=(q == 3))
            em.wait("tensor", sV, v_sq)
            for ds in range(NDS):
                mm(pS1[:, :], ones_bf[:, 0:1], A_xmcb[:, ds * TT:(ds + 1) * TT], start=(ds == 0),
                   stop=(ds == NDS - 1))
            ts = 0
            for ds in range(NDS):
                ts = mm(pS2[:, :], ones_bf[:, 0:1], A_xsq[:, ds * TT:(ds + 1) * TT], start=(ds == 0),
                        stop=(ds == NDS - 1), inc=(ds == NDS - 1))
            v_r = ln_rows(ts)
            a_seed["bc"] = ln_bcast(v_r, A_abc, A_bbc)

        def a_tail_hook_b(t_h_val):
            em.wait("sync", sT, t_h_val)
            for i, (_, w_sb, w_sem) in enumerate(WLIST[:3]):
                em.wait("sync", sG, a_seed[("d2d", i)])
                dma("sync", w_sem, sb3(w_sb), bass.AP(d_wb, i * D * D, [[D, 128], [128 * D, NDS], [1, D]]))
            em.wait("vector", sV, a_seed["bc"])
            em.wait("vector", sT, t_h_val)
            va = 0
            for ds in range(NDS):
                vop(lambda e, ds=ds: e.tensor_tensor(A_xhat[:, ds * TT:(ds + 1) * TT],
                                                     A_xmcb[:, ds * TT:(ds + 1) * TT], A_abc, ALU.mult))
            for ds in range(NDS):
                va = vop(lambda e, ds=ds: e.tensor_tensor(A_xhat[:, ds * TT:(ds + 1) * TT],
                                                          A_xhat[:, ds * TT:(ds + 1) * TT], A_bbc, ALU.add),
                         inc=(ds == NDS - 1))
            a_seed["apply"] = va

        f_phase(xT, d_xmid, reload_weights=False, bar=barrier,
                pre0=dict(xcb=v_xcb0, bc=bc0, apply=v_ap0),
                tail_hook_a=a_tail_hook_a, tail_hook_b=a_tail_hook_b, mid_hook=a_mid_hook)
        barrier()

        v_bvb = v_bvb0
        v_xmcb = {0: a_seed["xmcb"]}
        em.wait("sync", sG, a_seed[("d2d", 3)])
        dma("sync", sWo, sb3(A_wo), bass.AP(d_wb, 3 * D * D, [[D, 128], [128 * D, NDS], [1, D]]))

        dma("sync", sX, sb3(EXPT_sb, 4), bass.AP(EXPT, 0, [[TT, 128], [128 * TT, 4], [1, TT]]))
        dma("sync", sX, sb3(LT_sb, 4), bass.AP(LT01, 0, [[TT, 128], [128 * TT, 4], [1, TT]]))
        dma("sync", sX, A_eye, EYE.ap())
        v_acons = sX.v
        dma("gpsimd", sB, A_decbc, d_rows.ap()[2:3, :].partition_broadcast(128))
        v_bc2 = sB.v
        for tb in range(4):
            v_bc2 = dma("sync", sB, A_deckd[:, tb:tb + 1], bass.AP(d_rows, 3 * TT + tb * 128, [[1, 128], [1, 1]]))
        def emit_mask_and_zero():
            em.wait("vector", sX, v_acons)
            v_me = vop(lambda e: e.tensor_scalar(EXPT_sb, EXPT_sb, small[:, C_LOGD:C_LOGD + 1], None,
                                                 ALU.mult), inc=True)
            em.wait("scalar", sV, v_me)
            s_mex = sop(lambda e: e.activation(EXPT_sb, EXPT_sb, AF.Exp), inc=True)
            em.wait("vector", sS, s_mex)
            vop(lambda e: e.tensor_tensor(A_maskT, EXPT_sb, LT_sb, ALU.mult), inc=True)
            vop(lambda e: e.memset(A_M, 0.0))
            return vop(lambda e: e.memset(A_Mb, 0.0), inc=True)

        v_apply2 = {-1: 0, 0: a_seed["apply"]}
        t_qkv_last = {-1: 0}
        b_bc2 = {0: a_seed["bc"]}
        t_stats2 = {}
        bank_free = [0, 0, 0, 0, 0]
        s_mb_last = 0
        v_kd_prev = 0
        w_qdt_prev = 0

        def slot_at(bj, bi):
            return bi * (bi + 1) // 2 + bj

        def a_prep_cast(sc):
            em.wait("gpsimd", sV, v_apply2[sc - 1])
            v_xmcb[sc] = dma("gpsimd", sG, sb3(A_xmcb), dram_tile(d_xmid, sc * TT))

        def a_prep_stats_v(sc):
            em.wait("vector", sG, v_xmcb[sc])
            v_sq = 0
            for q in range(4):
                v_sq = vop(lambda e, q=q: e.tensor_tensor(A_xsq[:, q * 1024:(q + 1) * 1024],
                                                          A_xmcb[:, q * 1024:(q + 1) * 1024],
                                                          A_xmcb[:, q * 1024:(q + 1) * 1024], ALU.mult),
                           inc=(q == 3))
            return v_sq

        def a_prep_stats_t(sc, v_sq):
            em.wait("tensor", sV, v_sq)
            for ds in range(NDS):
                mm(pS1[:, :], ones_bf[:, 0:1], A_xmcb[:, ds * TT:(ds + 1) * TT], start=(ds == 0),
                   stop=(ds == NDS - 1))
            ts = 0
            for ds in range(NDS):
                ts = mm(pS2[:, :], ones_bf[:, 0:1], A_xsq[:, ds * TT:(ds + 1) * TT], start=(ds == 0),
                        stop=(ds == NDS - 1), inc=(ds == NDS - 1))
            t_stats2[sc] = ts

        def a_prep_apply(sc):
            em.wait("vector", sV, b_bc2[sc])
            em.wait("vector", sT, t_qkv_last[sc - 1])
            va = 0
            for ds in range(NDS):
                vop(lambda e, ds=ds: e.tensor_tensor(A_xhat[:, ds * TT:(ds + 1) * TT],
                                                     A_xmcb[:, ds * TT:(ds + 1) * TT], A_abc, ALU.mult))
            for ds in range(NDS):
                va = vop(lambda e, ds=ds: e.tensor_tensor(A_xhat[:, ds * TT:(ds + 1) * TT],
                                                          A_xhat[:, ds * TT:(ds + 1) * TT], A_bbc, ALU.add),
                         inc=(ds == NDS - 1))
            v_apply2[sc] = va

        for sc in range(NT):
            em.wait("tensor", sV, v_apply2[sc])
            em.wait("tensor", sWq, 16)
            for do in range(NDS):
                bi = do % 5
                em.wait("tensor", sV, bank_free[bi])
                t_q = 0
                for ds in range(NDS):
                    t_q = mm(pbk[bi][:, :], A_wq[:, ds * 1024 + do * 128:ds * 1024 + (do + 1) * 128],
                             A_xhat[:, ds * TT:(ds + 1) * TT], start=(ds == 0), stop=(ds == NDS - 1),
                             inc=(ds == NDS - 1))
                em.wait("vector", sT, t_q)
                bank_free[bi] = vop(lambda e, do=do, bi=bi: e.tensor_scalar(
                    A_qT[:, do * TT:(do + 1) * TT], pbk[bi][:, :], A_bq[:, do:do + 1], None, ALU.add), inc=True)
            em.wait("vector", sW, w_qdt_prev)
            em.wait("vector", sB, v_bc2)
            v_qd = 0
            for do in range(NDS):
                v_qd = vop(lambda e, do=do: e.tensor_tensor(A_qdT[:, do * TT:(do + 1) * TT],
                                                            A_qT[:, do * TT:(do + 1) * TT], A_decbc, ALU.mult),
                           inc=(do == NDS - 1))
            em.wait("sync", sV, v_qd)
            w_qdt_prev = dma("sync", sW, dram_tile(d_qdt, sc * TT), sb3(A_qdT))

            em.wait("tensor", sWk, 16)
            for do in range(NDS):
                bi = do % 5
                em.wait("tensor", sV, bank_free[bi])
                t_k = 0
                for ds in range(NDS):
                    t_k = mm(pbk[bi][:, :], A_wk[:, ds * 1024 + do * 128:ds * 1024 + (do + 1) * 128],
                             A_xhat[:, ds * TT:(ds + 1) * TT], start=(ds == 0), stop=(ds == NDS - 1),
                             inc=(ds == NDS - 1))
                em.wait("vector", sT, t_k)
                vop(lambda e, bi=bi, do=do: e.tensor_scalar(A_tmp, pbk[bi][:, :], A_bk[:, do:do + 1], 0.0,
                                                            ALU.add, ALU.min))
                v_k1 = vop(lambda e, bi=bi, do=do: e.tensor_scalar(A_tmp2, pbk[bi][:, :], A_bk[:, do:do + 1],
                                                                   0.0, ALU.add, ALU.max), inc=True)
                bank_free[bi] = v_k1
                em.wait("scalar", sV, v_k1)
                s_e = sop(lambda e: e.activation(A_tmp, A_tmp, AF.Exp), inc=True)
                em.wait("vector", sS, s_e)
                v_kT_last = vop(lambda e, do=do: e.tensor_tensor(A_kT[:, do * TT:(do + 1) * TT], A_tmp2, A_tmp,
                                                                 ALU.add), inc=True)

            if sc == 0:
                v_m0 = emit_mask_and_zero()
            if sc + 1 < NT:
                a_prep_cast(sc + 1)
                v_sq = a_prep_stats_v(sc + 1)
                a_prep_stats_t(sc + 1, v_sq)
                v_r = ln_rows(t_stats2[sc + 1])
                b_bc2[sc + 1] = ln_bcast(v_r, A_abc, A_bbc)

            em.wait("tensor", sWv, 16)
            em.wait("tensor", sG, v_bvb)
            for tb in range(4):
                for eh in range(2):
                    bi = 2 + (tb * 2 + eh) % 3
                    em.wait("tensor", sV, bank_free[bi])
                    for ds in range(NDS):
                        mm(pbk[bi][:, :], A_xhat[:, ds * TT + tb * 128:ds * TT + tb * 128 + 128],
                           A_wv[:, ds * 1024 + eh * 512:ds * 1024 + (eh + 1) * 512],
                           start=(ds == 0), stop=False)
                    t_v = mm(pbk[bi][:, :], ones_bf[0:1, 0:128], bvb_row[0:1, eh * 512:(eh + 1) * 512],
                             start=False, stop=True, inc=True)
                    em.wait("vector", sT, t_v)
                    bank_free[bi] = vop(lambda e, tb=tb, eh=eh, bi=bi: e.tensor_copy(
                        A_v[:, tb * 1024 + eh * 512:tb * 1024 + (eh + 1) * 512], pbk[bi][:, :]), inc=True)
            t_qkv_last[sc] = sT.v

            if sc + 1 < NT:
                a_prep_apply(sc + 1)


            def do_update(skip_mb):
                nonlocal s_mb_last
                for ds in range(NDS):
                    for eh in range(2):
                        pk = [0, 1, 4][(ds * 2 + eh) % 3]
                        em.wait("tensor", sV, bank_free[pk])
                        t_u = 0
                        for tb in range(4):
                            t_u = mm(pbk[pk][:, :], A_kd[:, tb * 1024 + ds * 128:tb * 1024 + (ds + 1) * 128],
                                     A_v[:, tb * 1024 + eh * 512:tb * 1024 + (eh + 1) * 512],
                                     start=(tb == 0), stop=(tb == 3), inc=(tb == 3))
                        em.wait("vector", sT, t_u)
                        v_u = vop(lambda e, ds=ds, eh=eh, pk=pk: e.scalar_tensor_tensor(
                            A_M[:, ds * 1024 + eh * 512:ds * 1024 + (eh + 1) * 512],
                            A_M[:, ds * 1024 + eh * 512:ds * 1024 + (eh + 1) * 512],
                            small[:, C_DECC:C_DECC + 1], pbk[pk][:, :], ALU.mult, ALU.add), inc=True)
                        bank_free[pk] = v_u
                        if not skip_mb:
                            em.wait("scalar", sV, v_u)
                            s_mb_last = sop(lambda e, ds=ds, eh=eh: e.copy(
                                A_Mb[:, ds * 1024 + eh * 512:ds * 1024 + (eh + 1) * 512],
                                A_M[:, ds * 1024 + eh * 512:ds * 1024 + (eh + 1) * 512]), inc=True)
                return sV.v

            em.wait("tensor", sV, v_kT_last)
            for bi_b in range(4):
                tb = bi_b
                em.wait("tensor", sV, v_kd_prev)
                t_tr = 0
                for ds in range(NDS):
                    t_tr = tr(pTr[:, ds * 128:(ds + 1) * 128],
                              A_kT[:, ds * TT + tb * 128:ds * TT + tb * 128 + 128], inc=(ds == NDS - 1))
                em.wait("vector", sT, t_tr)
                for half in range(2):
                    v_kd_prev = vop(lambda e, tb=tb, half=half: e.tensor_scalar(
                        A_kd[:, tb * 1024 + half * 512:tb * 1024 + (half + 1) * 512],
                        pTr[:, half * 512:(half + 1) * 512], A_deckd[:, tb:tb + 1], None, ALU.mult),
                        inc=(half == 1))
                pk = bi_b % 2
                em.wait("tensor", sV, bank_free[pk])
                t_sc = 0
                for bj in range(bi_b + 1):
                    for ds in range(NDS):
                        t_sc = mm(pbk[pk][:, bj * 128:(bj + 1) * 128],
                                  A_kT[:, ds * TT + bj * 128:ds * TT + bj * 128 + 128],
                                  A_qT[:, ds * TT + bi_b * 128:ds * TT + bi_b * 128 + 128],
                                  start=(bj == 0 and ds == 0), stop=(bj == bi_b and ds == NDS - 1),
                                  inc=(bj == bi_b and ds == NDS - 1))
                em.wait("vector", sT, t_sc)
                v_at = 0
                for bj in range(bi_b + 1):
                    v_at = vop(lambda e, bi_b=bi_b, bj=bj, pk=pk: e.tensor_tensor(
                        A_at[:, slot_at(bj, bi_b) * 128:(slot_at(bj, bi_b) + 1) * 128],
                        pbk[pk][:, bj * 128:(bj + 1) * 128],
                        A_maskT[:, bj * TT + bi_b * 128:bj * TT + bi_b * 128 + 128], ALU.mult),
                        inc=(bj == bi_b))
                bank_free[pk] = v_at

            if sc == NT - 1:
                v_u_last = do_update(skip_mb=True)
                em.wait("gpsimd", sV, v_u_last)
                cc_st = dma("gpsimd", sG, bass.AP(cc_in, 0, [[D, 128], [128 * D, NDS], [1, D]]), sb3(A_M))
                em.wait("gpsimd", sG, cc_st)
                sC.v += 1
                em.on("gpsimd", lambda e, s=sC.h: e.collective_compute(
                    "AllGather", ALU.bypass, replica_groups=[[0, 1], [2, 3], [4, 5], [6, 7]],
                    ins=[cc_in.ap()], outs=[cc_out.ap()]).then_inc(s, 1))

            v_at_done = max(bank_free[0], bank_free[1])
            em.wait("tensor", sV, v_at_done)
            for es in range(NDS):
                pk = 2 + es % 3
                em.wait("tensor", sV, bank_free[pk])
                fresh = True
                if sc > 0:
                    em.wait("tensor", sS, s_mb_last)
                    for ds in range(NDS):
                        mm(pbk[pk][:, :], A_Mb[:, ds * 1024 + es * 128:ds * 1024 + (es + 1) * 128],
                           A_qdT[:, ds * TT:(ds + 1) * TT], start=fresh, stop=False)
                        fresh = False
                t_rd = 0
                for bi_b in range(4):
                    for bj in range(bi_b + 1):
                        t_rd = mm(pbk[pk][:, bi_b * 128:(bi_b + 1) * 128],
                                  A_v[:, bj * 1024 + es * 128:bj * 1024 + (es + 1) * 128],
                                  A_at[:, slot_at(bj, bi_b) * 128:(slot_at(bj, bi_b) + 1) * 128],
                                  start=fresh, stop=(bi_b == 3 and bj == bi_b),
                                  inc=(bi_b == 3 and bj == bi_b))
                        fresh = False
                rs = es % 2
                rslot = A_rstg[:, rs * 512:(rs + 1) * 512]
                qeng = "sync" if rs == 0 else "scalar"
                em.wait("vector", sT, t_rd)
                em.wait("vector", sRd[rs], sRd[rs].v)
                v_r2 = vop(lambda e, sl=rslot, pk=pk: e.tensor_copy(sl, pbk[pk][:, :]), inc=True)
                bank_free[pk] = v_r2
                em.wait(qeng, sV, v_r2)
                dma(qeng, sRd[rs], bass.AP(d_read, es * 128 * T + sc * TT, [[T, 128], [1, TT]]), rslot)

            if sc < NT - 1:
                do_update(skip_mb=False)

        barrier()

        for ds in range(NDS):
            dma("sync", sX, F_s1T[:, ds * H:(ds + 1) * H], bass.AP(d_s1t, ds * 128 * H, [[H, 128], [1, H]]))
        em.wait("scalar", sC, 1)
        v_m0ld = dma("scalar", sX, sb3(K_m0), bass.AP(cc_out, 0, [[D, 128], [128 * D, NDS], [1, D]]))

        def rtg_load(sc):
            p = sc % 2
            dma("gpsimd", sK[p], sb3(K_rtg[p]), dram_tile(d_read, sc * TT))
            return sK[p].v

        v_xm = {0: dma("sync", sKx, sb3(K_xm[0]), dram_tile(d_xmid, 0))}
        k_ld = {0: rtg_load(0), 1: rtg_load(1)}
        bank_free = [0, 0, 0, 0, 0]
        w_xp = {}
        v_y1 = {}
        for sc in range(NT):
            p = sc % 2
            em.wait("tensor", sK[p], k_ld[sc])
            em.wait("tensor", sWo, 16)
            for do in range(NDS):
                pk = do % 5
                em.wait("tensor", sV, bank_free[pk])
                t_o = 0
                for es in range(NDS):
                    t_o = mm(pbk[pk][:, :], K_wo[:, es * 1024 + do * 128:es * 1024 + (do + 1) * 128],
                             K_rtg[p][:, es * TT:(es + 1) * TT], start=(es == 0), stop=(es == NDS - 1),
                             inc=(es == NDS - 1))
                em.wait("vector", sT, t_o)
                if do == 0:
                    em.wait("vector", sKx, v_xm[sc])
                vop(lambda e, pk=pk, do=do: e.tensor_scalar(K_tmp, pbk[pk][:, :], A_bo[:, do:do + 1], None,
                                                            ALU.add))
                bank_free[pk] = vop(lambda e, do=do: e.tensor_tensor(
                    K_xm[0][:, do * TT:(do + 1) * TT], K_xm[0][:, do * TT:(do + 1) * TT], K_tmp, ALU.add),
                    inc=True)
            v_y1[sc] = sV.v
            em.wait("sync", sV, v_y1[sc])
            w_xp[sc] = dma("sync", sW, dram_tile(d_xpost, sc * TT), sb3(K_xm[0]))
            if sc + 2 < NT:
                em.wait("gpsimd", sV, v_y1[sc])
                k_ld[sc + 2] = rtg_load(sc + 2)
            if sc + 1 < NT:
                em.wait("sync", sW, w_xp[sc])
                v_xm[sc + 1] = dma("sync", sKx, sb3(K_xm[0]), dram_tile(d_xmid, (sc + 1) * TT))

        def qdt_load(sc):
            p = sc % 2
            dma("sync", sK[p], sb3(K_qdt[p]), dram_tile(d_qdt, sc * TT))
            return sK[p].v

        q_ld = {0: qdt_load(0), 1: qdt_load(1)}
        em.wait("sync", sW, w_xp[0])
        xb_ld = {0: dma("sync", sKx, sb3(K_xm[0]), dram_tile(d_xpost, 0))}
        em.wait("tensor", sX, v_m0ld)
        for sc in range(NT):
            p = sc % 2
            em.wait("tensor", sK[p], q_ld[sc])
            for es in range(NDS):
                pk = es % 5
                em.wait("tensor", sV, bank_free[pk])
                t_c = 0
                for ds in range(NDS):
                    t_c = mm(pbk[pk][:, :], K_m0[:, ds * 1024 + es * 128:ds * 1024 + (es + 1) * 128],
                             K_qdt[p][:, ds * TT:(ds + 1) * TT], start=(ds == 0), stop=(ds == NDS - 1),
                             inc=(ds == NDS - 1))
                em.wait("vector", sT, t_c)
                bank_free[pk] = vop(lambda e, es=es, pk=pk, sc=sc: e.tensor_scalar(
                    K_rfull[:, es * TT:(es + 1) * TT], pbk[pk][:, :], small[:, C_SCV + sc:C_SCV + sc + 1],
                    None, ALU.mult), inc=True)
            v_rf = sV.v
            em.wait("tensor", sV, v_rf)
            for do in range(NDS):
                pk = do % 5
                em.wait("tensor", sV, bank_free[pk])
                t_o = 0
                for es in range(NDS):
                    t_o = mm(pbk[pk][:, :], K_wo[:, es * 1024 + do * 128:es * 1024 + (do + 1) * 128],
                             K_rfull[:, es * TT:(es + 1) * TT], start=(es == 0), stop=(es == NDS - 1),
                             inc=(es == NDS - 1))
                em.wait("vector", sT, t_o)
                if do == 0:
                    em.wait("vector", sKx, xb_ld[sc])
                bank_free[pk] = vop(lambda e, do=do, pk=pk: e.tensor_tensor(
                    K_xm[0][:, do * TT:(do + 1) * TT], K_xm[0][:, do * TT:(do + 1) * TT], pbk[pk][:, :],
                    ALU.add), inc=True)
            v_fin = sV.v
            em.wait("sync", sV, v_fin)
            w_xp2 = dma("sync", sW, dram_tile(d_xpost, sc * TT), sb3(K_xm[0]))
            if sc + 2 < NT:
                q_ld[sc + 2] = qdt_load(sc + 2)
            if sc + 1 < NT:
                em.wait("sync", sW, w_xp2)
                xb_ld[sc + 1] = dma("sync", sKx, sb3(K_xm[0]), dram_tile(d_xpost, (sc + 1) * TT))
            if sc == 0:
                em.wait("gpsimd", sW, w_xp2)
                f2_xcb = dma("gpsimd", sG, sb3(F_xcb), dram_tile(d_xpost, 0))
                em.wait("scalar", sG, f2_xcb)
                s_sqf = 0
                for q in range(4):
                    s_sqf = sop(lambda e, q=q: e.square(F_xsq[:, q * 1024:(q + 1) * 1024],
                                                        F_xcb[:, q * 1024:(q + 1) * 1024]), inc=(q == 3))
                em.wait("tensor", sS, s_sqf)
                for ds in range(NDS):
                    mm(pS1[:, :], ones_bf[:, 0:1], F_xcb[:, ds * TT:(ds + 1) * TT], start=(ds == 0),
                       stop=(ds == NDS - 1))
                tsf = 0
                for ds in range(NDS):
                    tsf = mm(pS2[:, :], ones_bf[:, 0:1], F_xsq[:, ds * TT:(ds + 1) * TT], start=(ds == 0),
                             stop=(ds == NDS - 1), inc=(ds == NDS - 1))
                v_rf2 = ln_rows(tsf)
                f2_abc = F_ring[:, 0:512]
                f2_bbc = F_ring[:, 512:1024]
                f2_bc = ln_bcast(v_rf2, f2_abc, f2_bbc)
                em.wait("vector", sV, f2_bc)
                v_apf = 0
                for ds in range(NDS):
                    vop(lambda e, ds=ds, a=f2_abc: e.tensor_tensor(F_xhat[:, ds * TT:(ds + 1) * TT],
                                                                   F_xcb[:, ds * TT:(ds + 1) * TT], a, ALU.mult))
                for ds in range(NDS):
                    v_apf = vop(lambda e, ds=ds, b=f2_bbc: e.tensor_tensor(F_xhat[:, ds * TT:(ds + 1) * TT],
                                                                           F_xhat[:, ds * TT:(ds + 1) * TT], b,
                                                                           ALU.add), inc=(ds == NDS - 1))

        f_phase(d_xpost, outT, reload_weights=True, bar=barrier,
                pre0=dict(xcb=f2_xcb, bc=f2_bc, apply=v_apf))

        for eng in ENGS:
            for dsem in ALL_DMA:
                em.wait(eng, dsem, dsem.v)

        with nc.Block() as block:
            em.run(block)

    return nc


def kernel(**inputs):
    x = np.asarray(inputs["L0_attn_x"], np.float32)
    B, S, _ = x.shape
    assert (B, S) == (4, 4096)

    w1 = np.asarray(inputs["L0_ffn_w1"], np.float32)
    w2 = np.asarray(inputs["L0_ffn_w2"], np.float32)

    com = dict(
        wq=np.ascontiguousarray(np.asarray(inputs["L0_attn_Wq"], np.float32)),
        wk=np.ascontiguousarray(np.asarray(inputs["L0_attn_Wk"], np.float32)),
        wv=np.ascontiguousarray(np.asarray(inputs["L0_attn_Wv"], np.float32)),
        wo=np.ascontiguousarray(np.asarray(inputs["L0_attn_Wo"], np.float32)),
        bq=np.asarray(inputs["L0_attn_bq"], np.float32),
        bk=np.asarray(inputs["L0_attn_bk"], np.float32),
        bv=np.asarray(inputs["L0_attn_bv"], np.float32),
        bo=np.asarray(inputs["L0_attn_bo"], np.float32),
        w1t=np.ascontiguousarray(w1.T),
        w2t=np.ascontiguousarray(w2.T),
        b1=np.asarray(inputs["L0_ffn_b1"], np.float32),
        b2=np.asarray(inputs["L0_ffn_b2"], np.float32),
        expj=np.arange(TT, dtype=np.float32),
        ekd=(TT - 1 - np.arange(TT)).astype(np.float32),
        eye=np.eye(128).astype(BF),
    )
    jj = np.arange(TT)
    lt = (jj[:, None] < jj[None, :]).astype(np.float32)
    e_mat = (jj[None, :] - 1 - jj[:, None]).astype(np.float32)
    com["expt"] = np.where(lt > 0, e_mat, 0.0).astype(np.float32)
    com["lt01"] = lt

    dp = float(np.asarray(inputs["L0_attn_decay_param"], np.float32).reshape(()))

    in_maps = []
    for c in range(8):
        b, h = c // 2, c % 2
        im = dict(com)
        im["xT"] = np.ascontiguousarray(x[b, h * T:(h + 1) * T, :].T)
        im["dpar"] = np.asarray([dp, float(h)], np.float32)
        im["poff"] = np.asarray([(c ^ 1) * D * D], np.int32)
        in_maps.append(im)

    nc = build(dbg=DBG)
    res = run_bass_kernel_spmd(nc, in_maps, core_ids=list(range(8)))

    out = np.empty((B, S, D), np.float32)
    for c in range(8):
        b, h = c // 2, c % 2
        out[b, h * T:(h + 1) * T, :] = np.asarray(res.results[c]["outT"], np.float32).T
    if DBG:
        kernel.dbg = res.results
    return out


# version 16 (final)
# speedup vs baseline: 1.1617x; 1.0521x over previous
import numpy as np
import ml_dtypes
import concourse.bass as bass
import concourse.mybir as mybir
from concourse.bass_utils import run_bass_kernel_spmd
from contextlib import ExitStack

F32 = mybir.dt.float32
BF16 = mybir.dt.bfloat16
AF = mybir.ActivationFunctionType
ALU = mybir.AluOpType
BF = ml_dtypes.bfloat16

D, H, T, TT = 1024, 4096, 2048, 512
NT, NDS, NHS = T // TT, D // 128, H // 128
EPS = 1e-5
ENGS = ("sync", "gpsimd", "tensor", "scalar", "vector")
DBG = False


class Sem:
    def __init__(self, nc, name):
        self.h = nc.alloc_semaphore(name=name)
        self.v = 0


class Em:
    def __init__(self):
        self.streams = {e: [] for e in ENGS}

    def on(self, eng, fn):
        self.streams[eng].append(fn)

    def wait(self, eng, sem, val):
        if isinstance(val, tuple):
            sem2, val = (val[0], val[1])
            if val > 0:
                self.on(eng, lambda e, s=self.sB.h, v=val: e.wait_ge(s, v))
            return
        if val > 0:
            self.on(eng, lambda e, s=sem.h, v=val: e.wait_ge(s, v))

    def emit(self, eng, sem, n, fn):
        if sem is None:
            self.on(eng, lambda e, f=fn: f(e))
            return None
        sem.v += n
        self.on(eng, lambda e, f=fn, s=sem.h, m=n: f(e).then_inc(s, m))
        return sem.v

    def run(self, block):
        for eng in ENGS:
            fns = self.streams[eng]
            getattr(block, eng)(lambda e, fns=fns: [f(e) for f in fns])


def build(dbg=False):
    nc = bass.Bass("TRN2", target_bir_lowering=False)

    xT = nc.dram_tensor("xT", [D, T], F32, kind="ExternalInput")
    WQ = nc.dram_tensor("wq", [D, D], F32, kind="ExternalInput")
    WK = nc.dram_tensor("wk", [D, D], F32, kind="ExternalInput")
    WV = nc.dram_tensor("wv", [D, D], F32, kind="ExternalInput")
    WO = nc.dram_tensor("wo", [D, D], F32, kind="ExternalInput")
    BQ = nc.dram_tensor("bq", [D], F32, kind="ExternalInput")
    BK = nc.dram_tensor("bk", [D], F32, kind="ExternalInput")
    BV = nc.dram_tensor("bv", [D], F32, kind="ExternalInput")
    BO = nc.dram_tensor("bo", [D], F32, kind="ExternalInput")
    W1T = nc.dram_tensor("w1t", [D, H], F32, kind="ExternalInput")
    W2T = nc.dram_tensor("w2t", [H, D], F32, kind="ExternalInput")
    B1 = nc.dram_tensor("b1", [H], F32, kind="ExternalInput")
    B2 = nc.dram_tensor("b2", [D], F32, kind="ExternalInput")
    DPAR = nc.dram_tensor("dpar", [2], F32, kind="ExternalInput")
    EXPJ = nc.dram_tensor("expj", [TT], F32, kind="ExternalInput")
    EKD = nc.dram_tensor("ekd", [TT], F32, kind="ExternalInput")
    EXPT = nc.dram_tensor("expt", [TT, TT], F32, kind="ExternalInput")
    LT01 = nc.dram_tensor("lt01", [TT, TT], F32, kind="ExternalInput")
    EYE = nc.dram_tensor("eye", [128, 128], BF16, kind="ExternalInput")

    outT = nc.dram_tensor("outT", [D, T], F32, kind="ExternalOutput")

    sk = "ExternalOutput" if dbg else "Internal"
    d_xmid = nc.dram_tensor("d_xmid", [D, T], F32, kind=sk)
    d_xpost = nc.dram_tensor("d_xpost", [D, T], F32, kind=sk)
    d_read = nc.dram_tensor("d_read", [D, T], F32, kind=sk)
    d_qdt = nc.dram_tensor("d_qdt", [D, T], BF16, kind=sk)
    d_s1t = nc.dram_tensor("d_s1t", [D, H], BF16)
    d_s2t = nc.dram_tensor("d_s2t", [H, D], BF16)
    d_wb = nc.dram_tensor("d_wb", [4 * D, D], BF16)
    d_rows = nc.dram_tensor("d_rows", [4, TT], F32)
    d_small = nc.dram_tensor("d_small", [16], F32)
    cc_in = nc.dram_tensor("cc_in", [D, D], BF16)
    cc_out = nc.dram_tensor("cc_out", [2 * D, D], BF16)
    POFF = nc.dram_tensor("poff", [1], mybir.dt.int32, kind="ExternalInput")

    with ExitStack() as ctx:
        E = ctx.enter_context
        AW = 52750
        arena = E(nc.sbuf_tensor("arena", [128, AW], F32))
        poff_t = E(nc.sbuf_tensor("poff_t", [1, 1], mybir.dt.int32))
        ar = arena.ap()

        def reg(off, words, dtype=F32):
            assert off + words <= AW, (off, words)
            a = ar[:, off:off + words]
            return a.bitcast(BF16) if dtype == BF16 else a

        ones_bf = reg(0, 64, BF16)
        alpha1g = reg(64, 32)
        b1t = reg(96, 32)
        alpha2h = reg(128, 8)
        b2h = reg(136, 8)
        small = reg(144, 16)
        rows_r = reg(160, 1536)
        P_bq = reg(1696, 8)
        P_bk = reg(1704, 8)
        P_bo = reg(1712, 8)
        PB = 1720
        C_LOGD, C_FLAG, C_DECC, C_SCV = 0, 1, 2, 4
        sl0 = rows_r[0:1, 0:TT]
        bchl = rows_r[0:1, 0:512].bitcast(BF16)
        sl1 = rows_r[0:1, TT:2 * TT]
        sl2 = rows_r[0:1, 2 * TT:3 * TT]
        r_dec, r_ekd = sl0, sl1
        r_dp = rows_r[0:1, 2 * TT:2 * TT + 8]
        r_bar = rows_r[0:1, 3 * TT - 1:3 * TT]
        poff_sb = poff_t[0:1, 0:1]

        F_s1T = reg(PB + 0, 16384, BF16)
        F_s2T = reg(PB + 16384, 16384, BF16)
        F_xcb = reg(PB + 32768, 2048, BF16)
        F_xsq = reg(PB + 34816, 2048, BF16)
        F_xhat = reg(PB + 36864, 2048, BF16)
        F_h = reg(PB + 38912, 8192, BF16)
        F_ring = reg(PB + 47104, 1024)
        F_abc = reg(PB + 48128, 512)
        F_bbc = reg(PB + 48640, 512)
        F_tmp = reg(PB + 49152, 512)
        wstgP = [reg(PB + 38912, 2048, BF16), reg(PB + 40960, 2048, BF16)]
        awP = [reg(PB + 43008, 2048, BF16), reg(PB + 45056, 2048, BF16)]

        A_wq = reg(PB + 0, 4096, BF16)
        A_wk = reg(PB + 4096, 4096, BF16)
        A_wv = reg(PB + 8192, 4096, BF16)
        A_qT = reg(PB + 12288, 2048, BF16)
        A_kT = reg(PB + 14336, 2048, BF16)
        A_wo = reg(PB + 16384, 4096, BF16)
        A_M = reg(PB + 20480, 8192)
        A_Mb = reg(PB + 28672, 4096, BF16)
        A_xmcb = reg(PB + 32768, 2048, BF16)
        A_xsq = reg(PB + 34816, 2048, BF16)
        A_xhat = reg(PB + 36864, 2048, BF16)
        A_kd = reg(PB + 38912, 2048, BF16)
        A_qdT = reg(PB + 40960, 2048, BF16)
        A_at = reg(PB + 43008, 640, BF16)
        A_rstg = reg(PB + 43648, 1024)
        A_v = reg(PB + 44672, 2048, BF16)
        A_tmp = reg(PB + 46720, 512)
        A_tmp2 = reg(PB + 47232, 512)
        A_deckd = reg(PB + 47744, 4)
        A_eye = reg(PB + 47748, 64, BF16)
        A_abc = F_abc
        A_bbc = F_bbc
        A_maskT = reg(PB + 49152, 1024, BF16)
        A_decbc = reg(PB + 50176, 256, BF16)
        A_bvb = reg(PB + 50432, 512, BF16)
        A_bq, A_bk, A_bo = P_bq, P_bk, P_bo
        EXPT_sb = reg(PB + 20480, 2048)
        LT_sb = reg(PB + 22528, 2048)
        bvb_row = A_bvb[0:1, 0:1024]

        K_wo = A_wo
        K_m0 = reg(PB + 20480, 4096, BF16)
        K_rtg = [reg(PB + 24576, 4096), reg(PB + 28672, 4096)]
        K_xm = [reg(PB + 38912, 4096), reg(PB + 38912, 4096)]
        K_qdt = [reg(PB + 43008, 2048, BF16), reg(PB + 45056, 2048, BF16)]
        K_rfull = reg(PB + 48128, 2048, BF16)
        K_tmp = reg(PB + 50176, 512)

        pbk = [E(nc.psum_tensor(f"pb{i}", [128, 512], F32)) for i in range(4)]
        pTr = E(nc.psum_tensor("pTr", [128, 1024], BF16))
        pS1 = E(nc.psum_tensor("pS1", [1, 512], F32))
        pS2 = E(nc.psum_tensor("pS2", [1, 512], F32))
        pBC = E(nc.psum_tensor("pBC", [128, 512], F32))
        pbk.append(pBC)

        sT, sV, sS = Sem(nc, "sT"), Sem(nc, "sV"), Sem(nc, "sS")
        sX, sW, sB = Sem(nc, "sX"), Sem(nc, "sW"), Sem(nc, "sB")
        sG, sC = Sem(nc, "sG"), Sem(nc, "sC")
        sWq, sWk, sWv, sWo = (Sem(nc, n) for n in ("sWq", "sWk", "sWv", "sWo"))
        sR = [Sem(nc, "sR0"), Sem(nc, "sR1")]
        sRd = [Sem(nc, "sRd0"), Sem(nc, "sRd1"), Sem(nc, "sRd2"), Sem(nc, "sRd3")]
        sK = [Sem(nc, "sKa"), Sem(nc, "sKb")]
        sKx = Sem(nc, "sKx")

        em = Em()
        em.sB = sB

        def dma(eng, sem, out_ap, in_ap, slow=False):
            if slow:
                return em.emit(eng, sem, 16,
                               lambda e, ou=out_ap, ii=in_ap: e.dma_start(out=ou, in_=ii,
                                                                          allow_slow_non_contiguous=True))
            return em.emit(eng, sem, 16, lambda e, ou=out_ap, ii=in_ap: e.dma_start(out=ou, in_=ii))

        def mm(out_ap, lhsT, rhs, start, stop, inc=False):
            return em.emit("tensor", sT if inc else None, 1,
                           lambda e, ou=out_ap, l=lhsT, r=rhs, a=start, b=stop: e.matmul(ou, l, r, start=a, stop=b))

        def tr(out_ap, in_ap, inc=False):
            return em.emit("tensor", sT if inc else None, 1,
                           lambda e, ou=out_ap, ii=in_ap: e.transpose(ou, ii, A_eye))

        def vop(fn, inc=False):
            return em.emit("vector", sV if inc else None, 1, fn)

        def sop(fn, inc=False):
            return em.emit("scalar", sS if inc else None, 1, fn)

        ALL_DMA = (sX, sW, sB, sG, sWq, sWk, sWv, sWo, sR[0], sR[1], sRd[0], sRd[1], sRd[2], sRd[3], sK[0], sK[1], sKx)

        def barrier():
            sT.v += 1
            em.on("tensor", lambda e, s=sT.h: e.matmul(pS1[0:1, 0:1], ones_bf[:, 0:1], ones_bf[:, 0:1],
                                                       start=True, stop=True).then_inc(s, 1))
            vt = sT.v
            sS.v += 1
            em.on("scalar", lambda e, s=sS.h: e.copy(r_bar, r_bar).then_inc(s, 1))
            vs = sS.v
            sV.v += 1
            em.on("vector", lambda e, s=sV.h: e.engine_nop().then_inc(s, 1))
            vv = sV.v
            for eng in ENGS:
                if eng != "tensor":
                    em.wait(eng, sT, vt)
                if eng != "scalar":
                    em.wait(eng, sS, vs)
                if eng != "vector":
                    em.wait(eng, sV, vv)
                for dsem in ALL_DMA:
                    em.wait(eng, dsem, dsem.v)

        def dram_tile(t_dram, tok0, nds=NDS, ntok=TT, cols=T):
            return bass.AP(t_dram, tok0, [[cols, 128], [128 * cols, nds], [1, ntok]])

        def sb3(sb_ap, nds=NDS):
            return sb_ap.rearrange("p (a t) -> p a t", a=nds)

        def cast2(sem, sb, t_dram, tok0):
            dma("gpsimd", sem, sb3(sb[:, 0:4 * TT], 4),
                bass.AP(t_dram, tok0, [[T, 128], [128 * T, 4], [1, TT]]))
            return dma("gpsimd", sem, sb3(sb[:, 4 * TT:8 * TT], 4),
                       bass.AP(t_dram, tok0 + 4 * 128 * T, [[T, 128], [128 * T, 4], [1, TT]]))

        def ln_rows(t_stats):
            em.wait("vector", sT, t_stats)
            vop(lambda e: e.tensor_copy(sl0, pS1[:, :]))
            vop(lambda e: e.tensor_copy(sl1, pS2[:, :]))
            vop(lambda e: e.scalar_tensor_tensor(sl2, sl0, -1.0 / D, sl0, ALU.mult, ALU.mult))
            vop(lambda e: e.tensor_tensor(sl2, sl1, sl2, ALU.add))
            vop(lambda e: e.tensor_scalar(sl2, sl2, 1.0 / D, EPS, ALU.mult, ALU.add))
            v2 = vop(lambda e: e.reciprocal(sl2, sl2), inc=True)
            em.wait("scalar", sV, v2)
            s1 = sop(lambda e: e.activation(sl2, sl2, AF.Sqrt), inc=True)
            em.wait("vector", sS, s1)
            return vop(lambda e: e.scalar_tensor_tensor(sl1, sl0, -1.0 / D, sl2, ALU.mult, ALU.mult), inc=True)

        def ln_bcast(v_rows, abc, bbc):
            em.wait("sync", sV, v_rows)
            dma("sync", sB, d_rows.ap()[0:1, :], sl2)
            dma("sync", sB, d_rows.ap()[1:2, :], sl1)
            em.wait("sync", sB, sB.v)
            dma("sync", sB, abc, d_rows.ap()[0:1, :].partition_broadcast(128))
            bcv = dma("sync", sB, bbc, d_rows.ap()[1:2, :].partition_broadcast(128))
            sV.v += 0
            return ("B", bcv)

        v_xcb0 = cast2(sG, F_xcb, xT, 0)
        dma("sync", sX, r_dp[0:1, 0:2], DPAR.ap()[None, :])
        dma("sync", sX, poff_sb, POFF.ap()[None, :])
        dma("sync", sX, r_dec, EXPJ.ap()[None, :])
        dma("sync", sX, r_ekd, EKD.ap()[None, :])
        dma("sync", sX, b1t, bass.AP(B1, 0, [[1, 128], [128, 32]]), slow=True)
        dma("sync", sX, b2h, bass.AP(B2, 0, [[1, 128], [128, 8]]), slow=True)
        for t_d, dst in ((BQ, P_bq), (BK, P_bk), (BO, P_bo)):
            dma("sync", sX, dst, bass.AP(t_d, 0, [[1, 128], [128, 8]]), slow=True)
        v_ld0 = sX.v
        v_bvb0 = dma("gpsimd", sG, bvb_row, BV.ap()[None, :])

        em.wait("vector", sX, v_ld0)
        vop(lambda e: e.memset(ones_bf, 1.0))
        vop(lambda e: e.tensor_scalar(b2h, b2h, 0.5, None, ALU.mult), inc=True)

        em.wait("scalar", sX, v_ld0)
        sop(lambda e: e.activation(r_dp[0:1, 2:3], r_dp[0:1, 0:1], AF.Sigmoid))
        v_logd = sop(lambda e: e.activation(r_dp[0:1, 3:4], r_dp[0:1, 2:3], AF.Ln), inc=True)
        em.wait("sync", sS, v_logd)
        dma("sync", sB, d_small.ap()[None, 0:1], r_dp[0:1, 3:4])
        dma("sync", sB, d_small.ap()[None, 1:2], r_dp[0:1, 1:2])
        em.wait("sync", sB, sB.v)
        dma("sync", sB, small[:, C_LOGD:C_LOGD + 1], d_small.ap()[None, 0:1].partition_broadcast(128))
        v_bc0 = dma("sync", sB, small[:, C_FLAG:C_FLAG + 1], d_small.ap()[None, 1:2].partition_broadcast(128))

        em.wait("vector", sB, v_bc0)
        vop(lambda e: e.tensor_scalar(small[:, C_DECC:C_DECC + 1], small[:, C_LOGD:C_LOGD + 1], float(TT),
                                      None, ALU.mult))
        for i in range(4):
            vop(lambda e, i=i: e.tensor_scalar(small[:, C_SCV + i:C_SCV + i + 1], small[:, C_LOGD:C_LOGD + 1],
                                               float(TT * i), None, ALU.mult))
        vop(lambda e: e.tensor_scalar(r_dec, r_dec, small[0:1, C_LOGD:C_LOGD + 1], None, ALU.mult))
        v_pre = vop(lambda e: e.tensor_scalar(r_ekd, r_ekd, small[0:1, C_LOGD:C_LOGD + 1], None, ALU.mult),
                    inc=True)
        em.wait("scalar", sV, v_pre)
        sop(lambda e: e.activation(small[:, C_DECC:C_DECC + 1], small[:, C_DECC:C_DECC + 1], AF.Exp))
        for i in range(4):
            sop(lambda e, i=i: e.activation(small[:, C_SCV + i:C_SCV + i + 1], small[:, C_SCV + i:C_SCV + i + 1],
                                            AF.Exp))
        sop(lambda e: e.activation(r_dec, r_dec, AF.Exp))
        v_exp1 = sop(lambda e: e.activation(r_ekd, r_ekd, AF.Exp), inc=True)
        em.wait("vector", sS, v_exp1)
        for i in range(4):
            vop(lambda e, i=i: e.tensor_tensor(small[:, C_SCV + i:C_SCV + i + 1],
                                               small[:, C_SCV + i:C_SCV + i + 1],
                                               small[:, C_FLAG:C_FLAG + 1], ALU.mult), inc=(i == 3))
        em.wait("sync", sS, v_exp1)
        dma("sync", sB, d_rows.ap()[2:3, :], r_dec)
        v_decb = dma("sync", sB, d_rows.ap()[3:4, :], r_ekd)

        em.wait("vector", sG, v_xcb0)
        v_sq0 = 0
        for q in range(4):
            v_sq0 = vop(lambda e, q=q: e.tensor_tensor(F_xsq[:, q * 1024:(q + 1) * 1024],
                                                       F_xcb[:, q * 1024:(q + 1) * 1024],
                                                       F_xcb[:, q * 1024:(q + 1) * 1024], ALU.mult),
                        inc=(q == 3))
        em.wait("tensor", sV, v_sq0)
        for ds in range(NDS):
            mm(pS1[:, :], ones_bf[:, 0:1], F_xcb[:, ds * TT:(ds + 1) * TT], start=(ds == 0),
               stop=(ds == NDS - 1))
        t_st0 = 0
        for ds in range(NDS):
            t_st0 = mm(pS2[:, :], ones_bf[:, 0:1], F_xsq[:, ds * TT:(ds + 1) * TT], start=(ds == 0),
                       stop=(ds == NDS - 1), inc=(ds == NDS - 1))
        em.wait("vector", sB, v_decb)
        v_r0 = ln_rows(t_st0)
        bc0 = ln_bcast(v_r0, F_abc, F_bbc)
        em.wait("vector", sV, bc0)
        v_ap0 = 0
        for ds in range(NDS):
            vop(lambda e, ds=ds: e.tensor_tensor(F_xhat[:, ds * TT:(ds + 1) * TT],
                                                 F_xcb[:, ds * TT:(ds + 1) * TT], F_abc, ALU.mult))
        for ds in range(NDS):
            v_ap0 = vop(lambda e, ds=ds: e.tensor_tensor(F_xhat[:, ds * TT:(ds + 1) * TT],
                                                         F_xhat[:, ds * TT:(ds + 1) * TT], F_bbc, ALU.add),
                        inc=(ds == NDS - 1))

        s_sign = {}
        v_aw = {}
        t_alv = {}
        first_al = [True, True]
        t_al1 = 0
        t_al2 = 0
        for ci in range(16):
            p = ci % 2
            w2g = ci >= 8
            g = ci - 8
            wst = wstgP[p]
            aw = awP[p]
            if ci >= 2:
                em.wait("gpsimd", sS, s_sign[ci - 2])
                em.wait("gpsimd", sV, v_aw[ci - 2])
            if not w2g:
                dma("gpsimd", sG, wst, bass.AP(W1T, ci * 128 * H, [[H, 128], [1, H]]))
            else:
                dma("gpsimd", sG, sb3(wst, 4), bass.AP(W2T, g * 512 * D, [[D, 128], [128 * D, 4], [1, D]]))
            v_w = sG.v
            em.wait("scalar", sG, v_w)
            if not w2g:
                s_sign[ci] = sop(lambda e, ci=ci, wst=wst: e.sign(F_s1T[:, ci * H:(ci + 1) * H], wst), inc=True)
            else:
                s_sign[ci] = sop(lambda e, g=g, wst=wst: e.sign(F_s2T[:, g * 4096:(g + 1) * 4096], wst),
                                 inc=True)
            em.wait("vector", sG, v_w)
            if w2g and g == 0:
                em.wait("vector", sT, t_al1)
                v_al1 = vop(lambda e: e.tensor_scalar(alpha1g, pbk[0][:, 0:32], 1.0 / D, None, ALU.mult), inc=True)
            v_aw[ci] = vop(lambda e, aw=aw, wst=wst: e.scalar_tensor_tensor(aw[:, 0:H], wst, -1.0, wst,
                                                                            ALU.mult, ALU.max), inc=True)
            em.wait("tensor", sV, v_aw[ci])
            if not w2g:
                for hs in range(NHS):
                    t_al1 = mm(pbk[0][:, hs:hs + 1], aw[:, hs * 128:(hs + 1) * 128], ones_bf[:, 0:1],
                               start=first_al[0], stop=(ci == 7 and hs == NHS - 1), inc=(hs == NHS - 1))
                    first_al[0] = False
            else:
                if g == 0:
                    em.wait("tensor", sV, v_al1)
                for k in range(4):
                    for ds in range(NDS):
                        t_al2 = mm(pbk[0][:, ds:ds + 1], aw[:, k * 1024 + ds * 128:k * 1024 + (ds + 1) * 128],
                                   ones_bf[:, 0:1], start=first_al[1],
                                   stop=(g == 7 and k == 3 and ds == NDS - 1), inc=(k == 3 and ds == NDS - 1))
                        first_al[1] = False
            em.wait("sync", sS, s_sign[ci])
            if not w2g:
                dma("sync", sW, bass.AP(d_s1t, ci * 128 * H, [[H, 128], [1, H]]), F_s1T[:, ci * H:(ci + 1) * H])
            else:
                dma("sync", sW, bass.AP(d_s2t, g * 512 * D, [[D, 128], [128 * D, 4], [1, D]]),
                    sb3(F_s2T[:, g * 4096:(g + 1) * 4096], 4))
        em.wait("vector", sT, t_al2)
        vop(lambda e: e.tensor_scalar(alpha2h, pbk[0][:, 0:8], 0.5 / H, None, ALU.mult), inc=True)

        def f_phase(xin, xout, reload_weights, bar=None, cast0_gate=None, pre0=None,
                    tail_hook_a=None, tail_hook_b=None, mid_hook=None):

            v_apply = {-1: 0}
            t_h_last = {-1: 0}
            v_xcb = {}
            b_bc = {}
            t_stats = {}
            bank_free_v = [0, 0, 0, 0, 0]
            bank_free_s = [0, 0, 0, 0, 0]
            slot_store = [sG.v, sG.v]

            def prep_cast(t):
                em.wait("gpsimd", sV, v_apply[t - 1])
                if t == 0 and cast0_gate is not None:
                    em.wait("gpsimd", cast0_gate[0], cast0_gate[1])
                v_xcb[t] = cast2(sG, F_xcb, xin, t * TT)

            def prep_stats_v(t):
                em.wait("vector", sG, v_xcb[t])
                v_sq = 0
                for q in range(4):
                    v_sq = vop(lambda e, q=q: e.tensor_tensor(F_xsq[:, q * 1024:(q + 1) * 1024],
                                                              F_xcb[:, q * 1024:(q + 1) * 1024],
                                                              F_xcb[:, q * 1024:(q + 1) * 1024], ALU.mult),
                               inc=(q == 3))
                return v_sq

            def prep_stats_t(t, v_sq):
                em.wait("tensor", sV, v_sq)
                for ds in range(NDS):
                    mm(pS1[:, :], ones_bf[:, 0:1], F_xcb[:, ds * TT:(ds + 1) * TT], start=(ds == 0),
                       stop=(ds == NDS - 1))
                ts = 0
                for ds in range(NDS):
                    ts = mm(pS2[:, :], ones_bf[:, 0:1], F_xsq[:, ds * TT:(ds + 1) * TT], start=(ds == 0),
                            stop=(ds == NDS - 1), inc=(ds == NDS - 1))
                t_stats[t] = ts

            def prep_apply(t):
                em.wait("vector", sV, b_bc[t])
                em.wait("vector", sT, t_h_last[t - 1])
                va = 0
                for ds in range(NDS):
                    vop(lambda e, ds=ds: e.tensor_tensor(F_xhat[:, ds * TT:(ds + 1) * TT],
                                                         F_xcb[:, ds * TT:(ds + 1) * TT], F_abc, ALU.mult))
                for ds in range(NDS):
                    va = vop(lambda e, ds=ds: e.tensor_tensor(F_xhat[:, ds * TT:(ds + 1) * TT],
                                                              F_xhat[:, ds * TT:(ds + 1) * TT], F_bbc, ALU.add),
                             inc=(ds == NDS - 1))
                v_apply[t] = va

            if pre0 is None:
                prep_cast(0)
                v_sq0 = prep_stats_v(0)
                prep_stats_t(0, v_sq0)
                v_r0 = ln_rows(t_stats[0])
                b_bc[0] = ln_bcast(v_r0, F_abc, F_bbc)
                prep_apply(0)
            else:
                v_xcb[0] = pre0["xcb"]
                b_bc[0] = pre0["bc"]
                v_apply[0] = pre0["apply"]
            if bar is not None:
                bar()
            if reload_weights:
                for g in range(8):
                    dma("sync", sX, sb3(F_s2T[:, g * 4096:(g + 1) * 4096], 4),
                        bass.AP(d_s2t, g * 512 * D, [[D, 128], [128 * D, 4], [1, D]]))
            v_s2_ld = sX.v

            for t in range(NT):
                em.wait("tensor", sV, v_apply[t])
                for hs in range(NHS):
                    bi = hs % 5
                    em.wait("tensor", sS, bank_free_s[bi])
                    em.wait("tensor", sV, bank_free_v[bi])
                    t_h = 0
                    for ds in range(NDS):
                        t_h = mm(pbk[bi][:, :], F_s1T[:, ds * H + hs * 128:ds * H + (hs + 1) * 128],
                                 F_xhat[:, ds * TT:(ds + 1) * TT], start=(ds == 0), stop=(ds == NDS - 1),
                                 inc=(ds == NDS - 1))
                    em.wait("scalar", sT, t_h)
                    bank_free_s[bi] = sop(lambda e, hs=hs, bi=bi: e.activation(
                        F_h[:, hs * TT:(hs + 1) * TT], pbk[bi][:, :], AF.Gelu,
                        bias=b1t[:, hs:hs + 1], scale=alpha1g[:, hs:hs + 1]), inc=True)
                    if hs == 7:
                        if t + 1 < NT:
                            prep_cast(t + 1)
                            v_sq = prep_stats_v(t + 1)
                            prep_stats_t(t + 1, v_sq)
                            v_r = ln_rows(t_stats[t + 1])
                            b_bc[t + 1] = ln_bcast(v_r, F_abc, F_bbc)
                        elif tail_hook_a is not None:
                            tail_hook_a(v_apply[t])
                s_gelu_all = sS.v
                t_h_last[t] = sT.v

                for ds2 in range(2):
                    em.wait("sync", sG, slot_store[ds2 % 2])
                    dma("sync", sR[ds2 % 2], F_ring[:, (ds2 % 2) * 512:(ds2 % 2 + 1) * 512],
                        bass.AP(xin, ds2 * 128 * T + t * TT, [[T, 128], [1, TT]]))

                if reload_weights and t == 0:
                    em.wait("tensor", sX, v_s2_ld)
                em.wait("tensor", sS, s_gelu_all)
                for ds2 in range(NDS):
                    bi = ds2 % 5
                    em.wait("tensor", sS, bank_free_s[bi])
                    em.wait("tensor", sV, bank_free_v[bi])
                    t_o2 = 0
                    for hs in range(NHS):
                        t_o2 = mm(pbk[bi][:, :], F_s2T[:, hs * 1024 + ds2 * 128:hs * 1024 + (ds2 + 1) * 128],
                                  F_h[:, hs * TT:(hs + 1) * TT], start=(hs == 0), stop=(hs == NHS - 1),
                                  inc=(hs == NHS - 1))
                    sl = ds2 % 2
                    em.wait("vector", sT, t_o2)
                    em.wait("vector", sR[sl], sR[sl].v)
                    vop(lambda e, bi=bi, ds2=ds2: e.tensor_scalar(F_tmp, pbk[bi][:, :], alpha2h[:, ds2:ds2 + 1],
                                                                  b2h[:, ds2:ds2 + 1], ALU.mult, ALU.add))
                    v_ev = vop(lambda e, sl=sl: e.tensor_tensor(F_ring[:, sl * 512:(sl + 1) * 512],
                                                                F_ring[:, sl * 512:(sl + 1) * 512], F_tmp,
                                                                ALU.add), inc=True)
                    bank_free_v[bi] = v_ev
                    em.wait("gpsimd", sV, v_ev)
                    em.wait("gpsimd", sG, sG.v)
                    slot_store[sl] = dma("gpsimd", sG, bass.AP(xout, ds2 * 128 * T + t * TT, [[T, 128], [1, TT]]),
                                         F_ring[:, sl * 512:(sl + 1) * 512])
                    if ds2 + 2 < NDS:
                        em.wait("sync", sG, slot_store[sl])
                        dma("sync", sR[sl], F_ring[:, sl * 512:(sl + 1) * 512],
                            bass.AP(xin, (ds2 + 2) * 128 * T + t * TT, [[T, 128], [1, TT]]))
                    if ds2 == 1:
                        if t + 1 < NT:
                            prep_apply(t + 1)
                        elif tail_hook_b is not None:
                            tail_hook_b(t_h_last[t])
                    if ds2 == NDS - 1 and mid_hook is not None:
                        mid_hook(t)

        a_seed = {}
        WLIST = ((WQ, A_wq, sWq), (WK, A_wk, sWk), (WV, A_wv, sWv), (WO, A_wo, sWo))

        def w_d2d(i):
            a_seed[("d2d", i)] = dma("gpsimd", sG, bass.AP(d_wb, i * D * D, [[D, 128], [128 * D, NDS], [1, D]]),
                                     bass.AP(WLIST[i][0], 0, [[D, 128], [128 * D, NDS], [1, D]]))

        def a_mid_hook(t):
            if t < 3:
                w_d2d(t)

        def a_tail_hook_a(v_apply_last):
            w_d2d(3)
            em.wait("gpsimd", sV, v_apply_last)
            a_seed["xmcb"] = cast2(sG, A_xmcb, d_xmid, 0)
            em.wait("vector", sG, a_seed["xmcb"])
            v_sq = 0
            for q in range(4):
                v_sq = vop(lambda e, q=q: e.tensor_tensor(A_xsq[:, q * 1024:(q + 1) * 1024],
                                                          A_xmcb[:, q * 1024:(q + 1) * 1024],
                                                          A_xmcb[:, q * 1024:(q + 1) * 1024], ALU.mult),
                           inc=(q == 3))
            em.wait("tensor", sV, v_sq)
            for ds in range(NDS):
                mm(pS1[:, :], ones_bf[:, 0:1], A_xmcb[:, ds * TT:(ds + 1) * TT], start=(ds == 0),
                   stop=(ds == NDS - 1))
            ts = 0
            for ds in range(NDS):
                ts = mm(pS2[:, :], ones_bf[:, 0:1], A_xsq[:, ds * TT:(ds + 1) * TT], start=(ds == 0),
                        stop=(ds == NDS - 1), inc=(ds == NDS - 1))
            v_r = ln_rows(ts)
            a_seed["bc"] = ln_bcast(v_r, A_abc, A_bbc)

        def a_tail_hook_b(t_h_val):
            em.wait("sync", sT, t_h_val)
            for i, (_, w_sb, w_sem) in enumerate(WLIST[:3]):
                em.wait("sync", sG, a_seed[("d2d", i)])
                dma("sync", w_sem, sb3(w_sb), bass.AP(d_wb, i * D * D, [[D, 128], [128 * D, NDS], [1, D]]))
            em.wait("vector", sV, a_seed["bc"])
            em.wait("vector", sT, t_h_val)
            va = 0
            for ds in range(NDS):
                vop(lambda e, ds=ds: e.tensor_tensor(A_xhat[:, ds * TT:(ds + 1) * TT],
                                                     A_xmcb[:, ds * TT:(ds + 1) * TT], A_abc, ALU.mult))
            for ds in range(NDS):
                va = vop(lambda e, ds=ds: e.tensor_tensor(A_xhat[:, ds * TT:(ds + 1) * TT],
                                                          A_xhat[:, ds * TT:(ds + 1) * TT], A_bbc, ALU.add),
                         inc=(ds == NDS - 1))
            a_seed["apply"] = va

        f_phase(xT, d_xmid, reload_weights=False, bar=barrier,
                pre0=dict(xcb=v_xcb0, bc=bc0, apply=v_ap0),
                tail_hook_a=a_tail_hook_a, tail_hook_b=a_tail_hook_b, mid_hook=a_mid_hook)
        barrier()

        v_bvb = v_bvb0
        v_xmcb = {0: a_seed["xmcb"]}
        em.wait("sync", sG, a_seed[("d2d", 3)])
        dma("sync", sWo, sb3(A_wo), bass.AP(d_wb, 3 * D * D, [[D, 128], [128 * D, NDS], [1, D]]))

        dma("sync", sX, sb3(EXPT_sb, 4), bass.AP(EXPT, 0, [[TT, 128], [128 * TT, 4], [1, TT]]))
        dma("sync", sX, sb3(LT_sb, 4), bass.AP(LT01, 0, [[TT, 128], [128 * TT, 4], [1, TT]]))
        dma("sync", sX, A_eye, EYE.ap())
        v_acons = sX.v
        dma("gpsimd", sB, A_decbc, d_rows.ap()[2:3, :].partition_broadcast(128))
        v_bc2 = sB.v
        for tb in range(4):
            v_bc2 = dma("sync", sB, A_deckd[:, tb:tb + 1], bass.AP(d_rows, 3 * TT + tb * 128, [[1, 128], [1, 1]]))
        def emit_mask_and_zero():
            em.wait("vector", sX, v_acons)
            v_me = vop(lambda e: e.tensor_scalar(EXPT_sb, EXPT_sb, small[:, C_LOGD:C_LOGD + 1], None,
                                                 ALU.mult), inc=True)
            em.wait("scalar", sV, v_me)
            s_mex = sop(lambda e: e.activation(EXPT_sb, EXPT_sb, AF.Exp), inc=True)
            em.wait("vector", sS, s_mex)
            vop(lambda e: e.tensor_tensor(A_maskT, EXPT_sb, LT_sb, ALU.mult), inc=True)
            vop(lambda e: e.memset(A_M, 0.0))
            return vop(lambda e: e.memset(A_Mb, 0.0), inc=True)

        v_apply2 = {-1: 0, 0: a_seed["apply"]}
        t_qkv_last = {-1: 0}
        b_bc2 = {0: a_seed["bc"]}
        t_stats2 = {}
        bank_free = [0, 0, 0, 0, 0]
        s_mb_last = 0
        v_kd_prev = 0
        w_qdt_prev = 0

        def slot_at(bj, bi):
            return bi * (bi + 1) // 2 + bj

        def a_prep_cast(sc):
            em.wait("gpsimd", sV, v_apply2[sc - 1])
            v_xmcb[sc] = cast2(sG, A_xmcb, d_xmid, sc * TT)

        def a_prep_stats_v(sc):
            em.wait("vector", sG, v_xmcb[sc])
            v_sq = 0
            for q in range(4):
                v_sq = vop(lambda e, q=q: e.tensor_tensor(A_xsq[:, q * 1024:(q + 1) * 1024],
                                                          A_xmcb[:, q * 1024:(q + 1) * 1024],
                                                          A_xmcb[:, q * 1024:(q + 1) * 1024], ALU.mult),
                           inc=(q == 3))
            return v_sq

        def a_prep_stats_t(sc, v_sq):
            em.wait("tensor", sV, v_sq)
            for ds in range(NDS):
                mm(pS1[:, :], ones_bf[:, 0:1], A_xmcb[:, ds * TT:(ds + 1) * TT], start=(ds == 0),
                   stop=(ds == NDS - 1))
            ts = 0
            for ds in range(NDS):
                ts = mm(pS2[:, :], ones_bf[:, 0:1], A_xsq[:, ds * TT:(ds + 1) * TT], start=(ds == 0),
                        stop=(ds == NDS - 1), inc=(ds == NDS - 1))
            t_stats2[sc] = ts

        def a_prep_apply(sc):
            em.wait("vector", sV, b_bc2[sc])
            em.wait("vector", sT, t_qkv_last[sc - 1])
            va = 0
            for ds in range(NDS):
                vop(lambda e, ds=ds: e.tensor_tensor(A_xhat[:, ds * TT:(ds + 1) * TT],
                                                     A_xmcb[:, ds * TT:(ds + 1) * TT], A_abc, ALU.mult))
            for ds in range(NDS):
                va = vop(lambda e, ds=ds: e.tensor_tensor(A_xhat[:, ds * TT:(ds + 1) * TT],
                                                          A_xhat[:, ds * TT:(ds + 1) * TT], A_bbc, ALU.add),
                         inc=(ds == NDS - 1))
            v_apply2[sc] = va

        for sc in range(NT):
            em.wait("tensor", sV, v_apply2[sc])
            em.wait("tensor", sWq, 16)
            for do in range(NDS):
                bi = do % 5
                em.wait("tensor", sV, bank_free[bi])
                t_q = 0
                for ds in range(NDS):
                    t_q = mm(pbk[bi][:, :], A_wq[:, ds * 1024 + do * 128:ds * 1024 + (do + 1) * 128],
                             A_xhat[:, ds * TT:(ds + 1) * TT], start=(ds == 0), stop=(ds == NDS - 1),
                             inc=(ds == NDS - 1))
                em.wait("vector", sT, t_q)
                bank_free[bi] = vop(lambda e, do=do, bi=bi: e.tensor_scalar(
                    A_qT[:, do * TT:(do + 1) * TT], pbk[bi][:, :], A_bq[:, do:do + 1], None, ALU.add), inc=True)
            em.wait("vector", sW, w_qdt_prev)
            em.wait("vector", sB, v_bc2)
            v_qd = 0
            for do in range(NDS):
                v_qd = vop(lambda e, do=do: e.tensor_tensor(A_qdT[:, do * TT:(do + 1) * TT],
                                                            A_qT[:, do * TT:(do + 1) * TT], A_decbc, ALU.mult),
                           inc=(do == NDS - 1))
            em.wait("sync", sV, v_qd)
            w_qdt_prev = dma("sync", sW, dram_tile(d_qdt, sc * TT), sb3(A_qdT))

            em.wait("tensor", sWk, 16)
            for do in range(NDS):
                bi = do % 5
                em.wait("tensor", sV, bank_free[bi])
                t_k = 0
                for ds in range(NDS):
                    t_k = mm(pbk[bi][:, :], A_wk[:, ds * 1024 + do * 128:ds * 1024 + (do + 1) * 128],
                             A_xhat[:, ds * TT:(ds + 1) * TT], start=(ds == 0), stop=(ds == NDS - 1),
                             inc=(ds == NDS - 1))
                em.wait("vector", sT, t_k)
                vop(lambda e, bi=bi, do=do: e.tensor_scalar(A_tmp, pbk[bi][:, :], A_bk[:, do:do + 1], 0.0,
                                                            ALU.add, ALU.min))
                v_k1 = vop(lambda e, bi=bi, do=do: e.tensor_scalar(A_tmp2, pbk[bi][:, :], A_bk[:, do:do + 1],
                                                                   0.0, ALU.add, ALU.max), inc=True)
                bank_free[bi] = v_k1
                em.wait("scalar", sV, v_k1)
                s_e = sop(lambda e: e.activation(A_tmp, A_tmp, AF.Exp), inc=True)
                em.wait("vector", sS, s_e)
                v_kT_last = vop(lambda e, do=do: e.tensor_tensor(A_kT[:, do * TT:(do + 1) * TT], A_tmp2, A_tmp,
                                                                 ALU.add), inc=True)

            if sc == 0:
                v_m0 = emit_mask_and_zero()
            if sc + 1 < NT:
                a_prep_cast(sc + 1)
                v_sq = a_prep_stats_v(sc + 1)
                a_prep_stats_t(sc + 1, v_sq)
                v_r = ln_rows(t_stats2[sc + 1])
                b_bc2[sc + 1] = ln_bcast(v_r, A_abc, A_bbc)

            em.wait("tensor", sWv, 16)
            em.wait("tensor", sG, v_bvb)
            for tb in range(4):
                for eh in range(2):
                    bi = 2 + (tb * 2 + eh) % 3
                    em.wait("tensor", sV, bank_free[bi])
                    for ds in range(NDS):
                        mm(pbk[bi][:, :], A_xhat[:, ds * TT + tb * 128:ds * TT + tb * 128 + 128],
                           A_wv[:, ds * 1024 + eh * 512:ds * 1024 + (eh + 1) * 512],
                           start=(ds == 0), stop=False)
                    t_v = mm(pbk[bi][:, :], ones_bf[0:1, 0:128], bvb_row[0:1, eh * 512:(eh + 1) * 512],
                             start=False, stop=True, inc=True)
                    em.wait("vector", sT, t_v)
                    bank_free[bi] = vop(lambda e, tb=tb, eh=eh, bi=bi: e.tensor_copy(
                        A_v[:, tb * 1024 + eh * 512:tb * 1024 + (eh + 1) * 512], pbk[bi][:, :]), inc=True)
            t_qkv_last[sc] = sT.v

            if sc + 1 < NT:
                a_prep_apply(sc + 1)


            def do_update(skip_mb):
                nonlocal s_mb_last
                for ds in range(NDS):
                    for eh in range(2):
                        pk = [0, 1, 4][(ds * 2 + eh) % 3]
                        em.wait("tensor", sV, bank_free[pk])
                        t_u = 0
                        for tb in range(4):
                            t_u = mm(pbk[pk][:, :], A_kd[:, tb * 1024 + ds * 128:tb * 1024 + (ds + 1) * 128],
                                     A_v[:, tb * 1024 + eh * 512:tb * 1024 + (eh + 1) * 512],
                                     start=(tb == 0), stop=(tb == 3), inc=(tb == 3))
                        em.wait("vector", sT, t_u)
                        v_u = vop(lambda e, ds=ds, eh=eh, pk=pk: e.scalar_tensor_tensor(
                            A_M[:, ds * 1024 + eh * 512:ds * 1024 + (eh + 1) * 512],
                            A_M[:, ds * 1024 + eh * 512:ds * 1024 + (eh + 1) * 512],
                            small[:, C_DECC:C_DECC + 1], pbk[pk][:, :], ALU.mult, ALU.add), inc=True)
                        bank_free[pk] = v_u
                        if not skip_mb:
                            em.wait("scalar", sV, v_u)
                            s_mb_last = sop(lambda e, ds=ds, eh=eh: e.copy(
                                A_Mb[:, ds * 1024 + eh * 512:ds * 1024 + (eh + 1) * 512],
                                A_M[:, ds * 1024 + eh * 512:ds * 1024 + (eh + 1) * 512]), inc=True)
                return sV.v

            em.wait("tensor", sV, v_kT_last)
            for bi_b in range(4):
                tb = bi_b
                em.wait("tensor", sV, v_kd_prev)
                t_tr = 0
                for ds in range(NDS):
                    t_tr = tr(pTr[:, ds * 128:(ds + 1) * 128],
                              A_kT[:, ds * TT + tb * 128:ds * TT + tb * 128 + 128], inc=(ds == NDS - 1))
                em.wait("vector", sT, t_tr)
                for half in range(2):
                    v_kd_prev = vop(lambda e, tb=tb, half=half: e.tensor_scalar(
                        A_kd[:, tb * 1024 + half * 512:tb * 1024 + (half + 1) * 512],
                        pTr[:, half * 512:(half + 1) * 512], A_deckd[:, tb:tb + 1], None, ALU.mult),
                        inc=(half == 1))
                pk = bi_b % 2
                em.wait("tensor", sV, bank_free[pk])
                t_sc = 0
                for bj in range(bi_b + 1):
                    for ds in range(NDS):
                        t_sc = mm(pbk[pk][:, bj * 128:(bj + 1) * 128],
                                  A_kT[:, ds * TT + bj * 128:ds * TT + bj * 128 + 128],
                                  A_qT[:, ds * TT + bi_b * 128:ds * TT + bi_b * 128 + 128],
                                  start=(bj == 0 and ds == 0), stop=(bj == bi_b and ds == NDS - 1),
                                  inc=(bj == bi_b and ds == NDS - 1))
                em.wait("vector", sT, t_sc)
                v_at = 0
                for bj in range(bi_b + 1):
                    v_at = vop(lambda e, bi_b=bi_b, bj=bj, pk=pk: e.tensor_tensor(
                        A_at[:, slot_at(bj, bi_b) * 128:(slot_at(bj, bi_b) + 1) * 128],
                        pbk[pk][:, bj * 128:(bj + 1) * 128],
                        A_maskT[:, bj * TT + bi_b * 128:bj * TT + bi_b * 128 + 128], ALU.mult),
                        inc=(bj == bi_b))
                bank_free[pk] = v_at

            if sc == NT - 1:
                v_u_last = do_update(skip_mb=True)
                em.wait("gpsimd", sV, v_u_last)
                cc_st = dma("gpsimd", sG, bass.AP(cc_in, 0, [[D, 128], [128 * D, NDS], [1, D]]), sb3(A_M))
                em.wait("gpsimd", sG, cc_st)
                sC.v += 1
                em.on("gpsimd", lambda e, s=sC.h: e.collective_compute(
                    "AllGather", ALU.bypass, replica_groups=[[0, 1], [2, 3], [4, 5], [6, 7]],
                    ins=[cc_in.ap()], outs=[cc_out.ap()]).then_inc(s, 1))

            v_at_done = max(bank_free[0], bank_free[1])
            em.wait("tensor", sV, v_at_done)
            for es in range(NDS):
                pk = 2 + es % 3
                em.wait("tensor", sV, bank_free[pk])
                fresh = True
                if sc > 0:
                    em.wait("tensor", sS, s_mb_last)
                    for ds in range(NDS):
                        mm(pbk[pk][:, :], A_Mb[:, ds * 1024 + es * 128:ds * 1024 + (es + 1) * 128],
                           A_qdT[:, ds * TT:(ds + 1) * TT], start=fresh, stop=False)
                        fresh = False
                t_rd = 0
                for bi_b in range(4):
                    for bj in range(bi_b + 1):
                        t_rd = mm(pbk[pk][:, bi_b * 128:(bi_b + 1) * 128],
                                  A_v[:, bj * 1024 + es * 128:bj * 1024 + (es + 1) * 128],
                                  A_at[:, slot_at(bj, bi_b) * 128:(slot_at(bj, bi_b) + 1) * 128],
                                  start=fresh, stop=(bi_b == 3 and bj == bi_b),
                                  inc=(bi_b == 3 and bj == bi_b))
                        fresh = False
                rs = es % 2
                rslot = A_rstg[:, rs * 512:(rs + 1) * 512]
                qeng = "sync" if rs == 0 else "scalar"
                em.wait("vector", sT, t_rd)
                em.wait("vector", sRd[rs], sRd[rs].v)
                v_r2 = vop(lambda e, sl=rslot, pk=pk: e.tensor_copy(sl, pbk[pk][:, :]), inc=True)
                bank_free[pk] = v_r2
                em.wait(qeng, sV, v_r2)
                dma(qeng, sRd[rs], bass.AP(d_read, es * 128 * T + sc * TT, [[T, 128], [1, TT]]), rslot)

            if sc < NT - 1:
                do_update(skip_mb=False)

        barrier()

        for ds in range(NDS):
            dma("sync", sX, F_s1T[:, ds * H:(ds + 1) * H], bass.AP(d_s1t, ds * 128 * H, [[H, 128], [1, H]]))
        em.wait("sync", sC, 1)
        v_m0ld = dma("sync", sX, sb3(K_m0), bass.AP(cc_out, 0, [[D, 128], [128 * D, NDS], [1, D]]))

        def k_load(sc):
            p = sc % 2
            dma("sync", sK[p], sb3(K_qdt[p]), dram_tile(d_qdt, sc * TT))
            dma("sync", sK[p], sb3(K_rtg[p]), dram_tile(d_read, sc * TT))
            return sK[p].v

        v_xm = {0: dma("sync", sKx, sb3(K_xm[0]), dram_tile(d_xmid, 0))}
        k_ld = {0: k_load(0), 1: k_load(1)}
        bank_free = [0, 0, 0, 0, 0]
        v_corr_done = {}
        w_xp = {}
        for sc in range(NT):
            p = sc % 2
            em.wait("tensor", sK[p], k_ld[sc])
            em.wait("tensor", sX, v_m0ld)
            for es in range(NDS):
                pk = es % 4
                em.wait("tensor", sV, bank_free[pk])
                t_c = 0
                for ds in range(NDS):
                    t_c = mm(pbk[pk][:, :], K_m0[:, ds * 1024 + es * 128:ds * 1024 + (es + 1) * 128],
                             K_qdt[p][:, ds * TT:(ds + 1) * TT], start=(ds == 0), stop=(ds == NDS - 1),
                             inc=(ds == NDS - 1))
                em.wait("vector", sT, t_c)
                bank_free[pk] = vop(lambda e, es=es, pk=pk, sc=sc, p=p: e.scalar_tensor_tensor(
                    K_rfull[:, es * TT:(es + 1) * TT], pbk[pk][:, :], small[:, C_SCV + sc:C_SCV + sc + 1],
                    K_rtg[p][:, es * TT:(es + 1) * TT], ALU.mult, ALU.add), inc=True)
            v_rfull = sV.v
            em.wait("tensor", sV, v_rfull)
            em.wait("tensor", sWo, 16)
            for do in range(NDS):
                pk = do % 4
                em.wait("tensor", sV, bank_free[pk])
                t_o = 0
                for es in range(NDS):
                    t_o = mm(pbk[pk][:, :], K_wo[:, es * 1024 + do * 128:es * 1024 + (do + 1) * 128],
                             K_rfull[:, es * TT:(es + 1) * TT], start=(es == 0), stop=(es == NDS - 1),
                             inc=(es == NDS - 1))
                em.wait("vector", sT, t_o)
                if do == 0:
                    em.wait("vector", sKx, v_xm[sc])
                vop(lambda e, pk=pk, do=do: e.tensor_scalar(K_tmp, pbk[pk][:, :], A_bo[:, do:do + 1], None,
                                                            ALU.add))
                bank_free[pk] = vop(lambda e, do=do: e.tensor_tensor(
                    K_xm[0][:, do * TT:(do + 1) * TT], K_xm[0][:, do * TT:(do + 1) * TT], K_tmp, ALU.add),
                    inc=True)
            v_corr_done[sc] = sV.v
            em.wait("sync", sV, v_corr_done[sc])
            w_xp[sc] = dma("sync", sW, dram_tile(d_xpost, sc * TT), sb3(K_xm[0]))
            if sc + 2 < NT:
                k_ld[sc + 2] = k_load(sc + 2)
            if sc + 1 < NT:
                em.wait("sync", sW, w_xp[sc])
                v_xm[sc + 1] = dma("sync", sKx, sb3(K_xm[0]), dram_tile(d_xmid, (sc + 1) * TT))
            if sc == 0:
                em.wait("gpsimd", sW, w_xp[0])
                f2_xcb = cast2(sG, F_xcb, d_xpost, 0)
                em.wait("vector", sG, f2_xcb)
                v_sqf = 0
                for q in range(4):
                    v_sqf = vop(lambda e, q=q: e.tensor_tensor(F_xsq[:, q * 1024:(q + 1) * 1024],
                                                               F_xcb[:, q * 1024:(q + 1) * 1024],
                                                               F_xcb[:, q * 1024:(q + 1) * 1024], ALU.mult),
                                inc=(q == 3))
                em.wait("tensor", sV, v_sqf)
                for ds in range(NDS):
                    mm(pS1[:, :], ones_bf[:, 0:1], F_xcb[:, ds * TT:(ds + 1) * TT], start=(ds == 0),
                       stop=(ds == NDS - 1))
                tsf = 0
                for ds in range(NDS):
                    tsf = mm(pS2[:, :], ones_bf[:, 0:1], F_xsq[:, ds * TT:(ds + 1) * TT], start=(ds == 0),
                             stop=(ds == NDS - 1), inc=(ds == NDS - 1))
                v_rf = ln_rows(tsf)
                f2_abc = F_ring[:, 0:512]
                f2_bbc = F_ring[:, 512:1024]
                f2_bc = ln_bcast(v_rf, f2_abc, f2_bbc)
                em.wait("vector", sV, f2_bc)
                v_apf = 0
                for ds in range(NDS):
                    vop(lambda e, ds=ds, a=f2_abc: e.tensor_tensor(F_xhat[:, ds * TT:(ds + 1) * TT],
                                                                   F_xcb[:, ds * TT:(ds + 1) * TT], a, ALU.mult))
                for ds in range(NDS):
                    v_apf = vop(lambda e, ds=ds, b=f2_bbc: e.tensor_tensor(F_xhat[:, ds * TT:(ds + 1) * TT],
                                                                           F_xhat[:, ds * TT:(ds + 1) * TT], b,
                                                                           ALU.add), inc=(ds == NDS - 1))

        f_phase(d_xpost, outT, reload_weights=True, bar=barrier,
                pre0=dict(xcb=f2_xcb, bc=f2_bc, apply=v_apf))

        for eng in ENGS:
            for dsem in ALL_DMA:
                em.wait(eng, dsem, dsem.v)

        with nc.Block() as block:
            em.run(block)

    return nc


def kernel(**inputs):
    x = np.asarray(inputs["L0_attn_x"], np.float32)
    B, S, _ = x.shape
    assert (B, S) == (4, 4096)

    w1 = np.asarray(inputs["L0_ffn_w1"], np.float32)
    w2 = np.asarray(inputs["L0_ffn_w2"], np.float32)

    com = dict(
        wq=np.ascontiguousarray(np.asarray(inputs["L0_attn_Wq"], np.float32)),
        wk=np.ascontiguousarray(np.asarray(inputs["L0_attn_Wk"], np.float32)),
        wv=np.ascontiguousarray(np.asarray(inputs["L0_attn_Wv"], np.float32)),
        wo=np.ascontiguousarray(np.asarray(inputs["L0_attn_Wo"], np.float32)),
        bq=np.asarray(inputs["L0_attn_bq"], np.float32),
        bk=np.asarray(inputs["L0_attn_bk"], np.float32),
        bv=np.asarray(inputs["L0_attn_bv"], np.float32),
        bo=np.asarray(inputs["L0_attn_bo"], np.float32),
        w1t=np.ascontiguousarray(w1.T),
        w2t=np.ascontiguousarray(w2.T),
        b1=np.asarray(inputs["L0_ffn_b1"], np.float32),
        b2=np.asarray(inputs["L0_ffn_b2"], np.float32),
        expj=np.arange(TT, dtype=np.float32),
        ekd=(TT - 1 - np.arange(TT)).astype(np.float32),
        eye=np.eye(128).astype(BF),
    )
    jj = np.arange(TT)
    lt = (jj[:, None] < jj[None, :]).astype(np.float32)
    e_mat = (jj[None, :] - 1 - jj[:, None]).astype(np.float32)
    com["expt"] = np.where(lt > 0, e_mat, 0.0).astype(np.float32)
    com["lt01"] = lt

    dp = float(np.asarray(inputs["L0_attn_decay_param"], np.float32).reshape(()))

    in_maps = []
    for c in range(8):
        b, h = c // 2, c % 2
        im = dict(com)
        im["xT"] = np.ascontiguousarray(x[b, h * T:(h + 1) * T, :].T)
        im["dpar"] = np.asarray([dp, float(h)], np.float32)
        im["poff"] = np.asarray([(c ^ 1) * D * D], np.int32)
        in_maps.append(im)

    nc = build(dbg=DBG)
    res = run_bass_kernel_spmd(nc, in_maps, core_ids=list(range(8)))

    out = np.empty((B, S, D), np.float32)
    for c in range(8):
        b, h = c // 2, c % 2
        out[b, h * T:(h + 1) * T, :] = np.asarray(res.results[c]["outT"], np.float32).T
    if DBG:
        kernel.dbg = res.results
    return out
